# version 05
# baseline (speedup 1.0000x reference)
import numpy as np
import ml_dtypes
from contextlib import ExitStack

import concourse.bass as bass
import concourse.mybir as mybir
from concourse.bass_utils import run_bass_kernel_spmd

F32 = mybir.dt.float32
F32R = mybir.dt.float32r
BF16 = mybir.dt.bfloat16
I32 = mybir.dt.int32
AF = mybir.ActivationFunctionType
ALU = mybir.AluOpType
AX = mybir.AxisListType
BF = ml_dtypes.bfloat16

NCORE = 8
T = 2048
D = 1024
NT = 16
NCK = 8
H, HD = 16, 64
EPS = 1e-5

ENGS = ("vector", "scalar", "tensor", "gpsimd", "sync")


class DmaSem:
    def __init__(self, name):
        self.name = name
        self.n = 0
        self.h = None


class Prog:

    def __init__(self):
        self.nc = bass.Bass("TRN2", target_bir_lowering=False)
        self.steps = []
        self.ecount = {e: 0 for e in ENGS}
        self.dsems = {}
        self.stack = ExitStack()
        self.bank_tok = {}
        self._junk = {}

    def dma_sem(self, name):
        s = DmaSem(name)
        self.dsems[name] = s
        return s

    def sb(self, name, shape, dtype, **kw):
        return self.stack.enter_context(self.nc.sbuf_tensor(name, shape, dtype, **kw))

    def ps(self, name, shape=(128, 512), dtype=F32):
        return self.stack.enter_context(self.nc.psum_tensor(name, list(shape), dtype))

    def op(self, eng, fn, deps=(), flush=False):
        self.ecount[eng] += 1
        tok = ("e", eng, self.ecount[eng])
        self.steps.append((eng, fn, tuple(d for d in deps if d is not None)
                           + (("f", eng, self.ecount[eng] - 1),) * flush, None, tok))
        return tok

    def fence(self, eng, deps=()):
        jt = self._junk.get(eng)
        if jt is None:
            jt = self.sb(f"junk_{eng}", [1, 8], F32)
            self._junk[eng] = jt
        return self.op(eng, lambda e, jt=jt: e.memzero(jt[:, :]), deps)

    def dma(self, eng, fn, sem, deps=()):
        deps = tuple(d for d in deps if d is not None)
        if sem.n:
            deps += (("d", sem.name, sem.n),)
        sem.n += 1
        tok = ("d", sem.name, sem.n)
        self.steps.append((eng, fn, deps, sem, tok))
        return tok

    def mm(self, psum_ap, lhsT_fn, rhs_fn, nk, deps=(), start=True, stop=True,
           extras=()):
        name = psum_ap.tensor.name

        def fn(e):
            last = None
            for k in range(nk):
                last = e.matmul(psum_ap, lhsT_fn(k), rhs_fn(k),
                                start=(start and k == 0),
                                stop=(stop and not extras and k == nk - 1))
            for j, ex in enumerate(extras):
                last = ex(e)
            return last
        deps = tuple(deps)
        if name in self.bank_tok:
            deps += (self.bank_tok[name],)
        return self.op("tensor", fn, deps)

    def evac(self, psum_name, tok):
        self.bank_tok[psum_name] = tok

    def build(self):
        nc = self.nc
        esems = {}
        for e in ENGS:
            esems[e] = self.stack.enter_context(nc.semaphore(f"tok_{e}"))
        for s in self.dsems.values():
            s.h = self.stack.enter_context(nc.semaphore(s.name))
        block = self.stack.enter_context(nc.Block())
        steps = self.steps
        dsems = self.dsems
        ecount = self.ecount

        def run_stream(myname, eng):
            hw = {}
            for (ename, fn, deps, dsem, tok) in steps:
                if ename != myname:
                    continue
                for dep in deps:
                    kind, key, cnt = dep
                    if cnt == 0:
                        continue
                    if kind == "f":
                        sem, val = esems[myname], cnt
                    elif kind == "e":
                        if key == myname:
                            continue
                        sem, val = esems[key], cnt
                    else:
                        sem, val = dsems[key].h, 16 * cnt
                    hwk = {"e": key, "d": "d" + key, "f": myname}[kind]
                    if hw.get(hwk, -1) < val:
                        eng.wait_ge(sem, val)
                        hw[hwk] = val
                inst = fn(eng)
                if dsem is not None:
                    inst.then_inc(dsem.h, 16)
                else:
                    inst.then_inc(esems[ename], 1)
            if myname == "gpsimd":
                for e in ENGS:
                    if ecount[e] and e != "gpsimd":
                        eng.wait_ge(esems[e], ecount[e])
                for s in dsems.values():
                    if s.n:
                        eng.wait_ge(s.h, 16 * s.n)

        for name, setter in (("gpsimd", block.gpsimd), ("vector", block.vector),
                             ("scalar", block.scalar), ("tensor", block.tensor),
                             ("sync", block.sync)):
            if name == "gpsimd" or any(s[0] == name for s in steps):
                setter(lambda eng, name=name: run_stream(name, eng))
        self.stack.close()
        return nc



def _mslabs(wT, width):
    i, o = wT.shape
    return np.ascontiguousarray(wT.reshape(i, o // width, width).transpose(1, 0, 2))


def prep_host(inputs):
    ws = {}
    wqkvT = np.ascontiguousarray(np.asarray(inputs["L0_attn_w_qkv"], np.float32).T)
    ws["wq_slab"] = _mslabs(np.ascontiguousarray(wqkvT[:, 0:1024]), 128)
    ws["wkv_slab"] = _mslabs(np.ascontiguousarray(wqkvT[:, 1024:3072]), 512)
    ws["woT_slab"] = _mslabs(np.ascontiguousarray(np.asarray(inputs["L0_attn_w_out"], np.float32).T), 512)
    ws["b_out_row"] = np.ascontiguousarray(np.asarray(inputs["L0_attn_b_out"], np.float32).reshape(1, 1024))
    ws["wffnT"] = np.ascontiguousarray(np.asarray(inputs["L0_ffn_weight"], np.float32).T)
    ws["gamma_b"] = np.ascontiguousarray(np.broadcast_to(np.asarray(inputs["L0_ffn_gamma"], np.float32), (128, 1024)))
    ws["beta_b"] = np.ascontiguousarray(np.broadcast_to(np.asarray(inputs["L0_ffn_beta"], np.float32), (128, 1024)))
    w1s = _mslabs(np.ascontiguousarray(np.asarray(inputs["L1_attn_W"], np.float32)), 128)
    ws["w1_slab"] = np.ascontiguousarray(
        w1s.reshape(6, 4, 1024, 128).transpose(0, 2, 1, 3).reshape(6, 1024, 512))
    ws["b1_fm"] = np.ascontiguousarray(np.asarray(inputs["L1_attn_b"], np.float32).reshape(24, 128).T)
    ws["rwT"] = np.ascontiguousarray(np.asarray(inputs["L1_ffn_router_w"], np.float32).T)
    ws["cb_b"] = np.ascontiguousarray(np.broadcast_to(np.asarray(inputs["L1_ffn_correction_bias"], np.float32), (128, 16)))
    gw = np.asarray(inputs["L1_ffn_gate_w"], np.float32)
    uw = np.asarray(inputs["L1_ffn_up_w"], np.float32)
    dw = np.asarray(inputs["L1_ffn_down_w"], np.float32)
    gw_s = gw.reshape(16, 1024, 2, 128).transpose(0, 2, 1, 3)
    uw_s = uw.reshape(16, 1024, 2, 128).transpose(0, 2, 1, 3)
    ws["guw_slab"] = np.ascontiguousarray(np.stack([gw_s, uw_s], axis=1))
    ws["dw_slab"] = np.ascontiguousarray(dw.reshape(16, 256, 2, 512).transpose(0, 2, 1, 3))
    ws["sgT_slab"] = _mslabs(np.ascontiguousarray(np.asarray(inputs["L1_ffn_shared_gate_w"], np.float32).T), 128)
    ws["suT_slab"] = _mslabs(np.ascontiguousarray(np.asarray(inputs["L1_ffn_shared_up_w"], np.float32).T), 128)
    ws["sdT_slab"] = _mslabs(np.ascontiguousarray(np.asarray(inputs["L1_ffn_shared_down_w"], np.float32).T), 512)
    sel = np.zeros((16, 16 * 128), np.float32)
    for e in range(16):
        sel[e, e * 128:(e + 1) * 128] = 1.0
    ws["onehot"] = sel
    selh = np.zeros((8, 16, 128), np.float32)
    for c in range(8):
        selh[c, 2 * c, 0:64] = 1.0
        selh[c, 2 * c + 1, 64:128] = 1.0
    ws["selhead"] = selh
    ws["ident"] = np.eye(128, dtype=np.float32)
    x = np.ascontiguousarray(np.asarray(inputs["L0_attn_x"], np.float32).reshape(NCORE * T, D))
    xs = [np.ascontiguousarray(x[c * T:(c + 1) * T]) for c in range(NCORE)]
    return ws, xs



def build_launch1():
    p = Prog()
    nc = p.nc
    x_d = nc.dram_tensor("x", [T, D], F32, kind="ExternalInput").ap()
    wq_d = nc.dram_tensor("wq_slab", [8, 1024, 128], F32, kind="ExternalInput").ap()
    wkv_d = nc.dram_tensor("wkv_slab", [4, 1024, 512], F32, kind="ExternalInput").ap()
    gam_d = nc.dram_tensor("gamma_b", [128, 1024], F32, kind="ExternalInput").ap()
    bet_d = nc.dram_tensor("beta_b", [128, 1024], F32, kind="ExternalInput").ap()
    q_o = nc.dram_tensor("q_fm", [128, 8 * T], BF16, kind="ExternalOutput").ap()
    xn2_o = nc.dram_tensor("xn2", [128, NT * 1024], F32, kind="ExternalOutput").ap()
    kv_o = nc.dram_tensor("kvp", [128, 520], F32, kind="ExternalOutput").ap()
    mx_o = nc.dram_tensor("maxloc", [1, 1], F32, kind="ExternalOutput").ap()
    bounce = nc.dram_tensor("bounce", [1, 128], F32).ap()

    x_tm = p.sb("x_tm", [128, NT * 1024], F32)
    xn_fm = p.sb("xn_fm", [128, 8 * T], BF16, is_transpose_dest=True)
    wq_sb = p.sb("wq_sb", [128, 8, 1024], BF16)
    wkv_sb = p.sb("wkv_sb", [128, 8, 2048], BF16)
    gam_sb = p.sb("gam_sb", [128, 1024], F32)
    bet_sb = p.sb("bet_sb", [128, 1024], F32)
    s1 = p.sb("s1", [128, NT], F32)
    s2 = p.sb("s2", [128, NT], F32)
    mean = p.sb("mean", [128, NT], F32)
    rstd = p.sb("rstd", [128, NT], F32)
    vscr = p.sb("vscr", [128, NT], F32)
    mean2 = p.sb("mean2", [128, NT], F32)
    rstd2 = p.sb("rstd2", [128, NT], F32)
    mxc = p.sb("mxc", [128, NT], F32)
    dump = p.sb("dump", [128, 1024], F32)
    cbw_scr = [dump[:, j * 512:(j + 1) * 512] for j in range(2)]
    xn_scr = [p.sb(f"xn_scr{j}", [128, 1024], F32) for j in range(2)]
    xnb_scr = [p.sb(f"xnb_scr{j}", [128, 1024], BF16) for j in range(2)]
    u_scr = [p.sb(f"u_scr{j}", [128, 1024], F32) for j in range(2)]
    ev_scr = [p.sb(f"ev_scr{j}", [128, 512], F32) for j in range(4)]
    q_strip = [p.sb(f"q_strip{j}", [128, 2048], BF16) for j in range(2)]
    k_t = p.sb("k_t", [128, 1024], BF16)
    v_t = p.sb("v_t", [128, 1024], BF16)
    ones_bf = p.sb("ones_bf", [128, 1], BF16)
    kvs = p.sb("kvs", [128, 520], F32)
    red = p.sb("red", [128, 1], F32)
    redT = p.sb("redT", [1, 128], F32)
    gmx = p.sb("gmx", [1, 1], F32)

    ps_q = [p.ps(f"ps_q{j}") for j in range(2)]
    ps_kA = [p.ps(f"ps_kA{j}") for j in range(2)]
    ps_kB = [p.ps(f"ps_kB{j}") for j in range(2)]
    ps_kv = p.ps("ps_kv")
    ps_ks = p.ps("ps_ks")

    dIN = p.dma_sem("dIN")
    dIN2 = p.dma_sem("dIN2")
    dTR = p.dma_sem("dTR")
    dOUT = p.dma_sem("dOUT")
    dBN = p.dma_sem("dBN")

    def xt(i):
        return x_tm[:, i * 1024:(i + 1) * 1024]

    def xnf(c, a, b):
        return xn_fm[:, c * T + a:c * T + b]

    t_x = p.dma("sync", lambda e: e.dma_start(
        out=x_tm[:, :].rearrange("p (i f) -> p i f", i=NT),
        in_=x_d.rearrange("(i p) f -> p i f", p=128)), dIN2)
    t_gam = p.dma("sync", lambda e: e.dma_start(out=gam_sb[:, :], in_=gam_d), dIN2)
    t_bet = p.dma("sync", lambda e: e.dma_start(out=bet_sb[:, :], in_=bet_d), dIN2)
    t_wq = [p.dma("gpsimd", lambda e, m=m: e.dma_start(
        out=wq_sb[:, :, m * 128:(m + 1) * 128],
        in_=wq_d[m].rearrange("(k p) c -> p k c", p=128)), dIN) for m in range(8)]
    t_wkv = [p.dma("gpsimd", lambda e, n=n: e.dma_start(
        out=wkv_sb[:, :, n * 512:(n + 1) * 512],
        in_=wkv_d[n].rearrange("(k p) c -> p k c", p=128)), dIN) for n in range(4)]
    t_ones = p.op("vector", lambda e: e.memset(ones_bf[:, :], 1.0))

    def ln_block(src_fn, src_deps, mean_t, rstd_t):
        toks = []
        for i in range(NT):
            dep_i = src_deps(i)
            toks.append(p.op("vector", lambda e, i=i: e.reduce_sum(
                s1[:, i:i + 1], src_fn(i), axis=AX.X), dep_i))
            toks.append(p.op("scalar", lambda e, i=i: e.activation(
                dump[:, :], src_fn(i), AF.Square, accum_out=s2[:, i:i + 1]), dep_i))
        t1 = p.op("vector", lambda e: e.tensor_scalar_mul(mean_t[:, :], s1[:, :], 1.0 / D), toks)
        t2 = p.op("vector", lambda e: e.tensor_scalar_mul(vscr[:, :], s2[:, :], 1.0 / D), toks)
        t3 = p.op("vector", lambda e: e.tensor_mul(s1[:, :], mean_t[:, :], mean_t[:, :]), (t1,), flush=True)
        t4 = p.op("vector", lambda e: e.tensor_sub(vscr[:, :], vscr[:, :], s1[:, :]), (t2, t3), flush=True)
        t5 = p.op("vector", lambda e: e.tensor_scalar_add(vscr[:, :], vscr[:, :], EPS), (t4,), flush=True)
        t6 = p.op("scalar", lambda e: e.sqrt(s2[:, :], vscr[:, :]), (t5,))
        t7 = p.op("vector", lambda e: e.reciprocal(rstd_t[:, :], s2[:, :]), (t6,), flush=True)
        return (t7,), toks

    t_ln1, ln1_tile_toks = ln_block(xt, lambda i: (t_x,), mean, rstd)

    tr_toks = []
    for i in range(NT):
        sc, scb = xn_scr[i % 2], xnb_scr[i % 2]
        war = (tr_toks[-2],) if len(tr_toks) >= 2 else ()
        tn = p.op("vector", lambda e, i=i, sc=sc: e.tensor_scalar(
            sc[:, :], xt(i), mean[:, i:i + 1], rstd[:, i:i + 1],
            ALU.subtract, ALU.mult), t_ln1 + (t_x,) + war)
        tc = p.op("gpsimd", lambda e, sc=sc, scb=scb: e.tensor_copy(scb[:, :], sc[:, :]), (tn,))
        tt = p.dma("sync", lambda e, i=i, scb=scb: e.dma_start_transpose(
            xn_fm[:, :].rearrange("p (c t) -> p c t", c=8)[:, :, i * 128:(i + 1) * 128],
            scb[:, :]), dTR, (tc,))
        tr_toks.append(tt)
    t_xn = tr_toks[-1]

    elu_war = {}
    elu_war0 = {}

    def elu1_evac(ps_ap, out_ap, j, deps):
        e0, e1 = ev_scr[2 * j], ev_scr[2 * j + 1]
        ta = p.op("vector", lambda e: e.tensor_scalar_min(e0[:, :], ps_ap, 0.0),
                  tuple(deps) + (elu_war0.get(j),))
        tb = p.op("scalar", lambda e: e.activation(e1[:, :], e0[:, :], AF.Exp),
                  (ta, elu_war.get(j)))
        tc = p.op("vector", lambda e: e.scalar_tensor_tensor(
            out_ap, ps_ap, 0.0, e1[:, :], ALU.max, ALU.add), (tb,))
        elu_war[j] = tc
        elu_war0[j] = tb
        return tc

    q_dma_toks = []
    for m in range(8):
        strip = q_strip[m % 2]
        strip_war = q_dma_toks[-2] if len(q_dma_toks) >= 2 else None
        ev_toks = []
        for t4 in range(4):
            psq = ps_q[t4 % 2]
            tm = p.mm(psq[:, :],
                      lambda k, m=m: wq_sb[:, k, m * 128:(m + 1) * 128],
                      lambda k, t4=t4: xnf(k, t4 * 512, (t4 + 1) * 512),
                      8, deps=(t_xn, t_wq[m]))
            te = elu1_evac(psq[:, :], strip[:, t4 * 512:(t4 + 1) * 512], t4 % 2,
                           (tm, strip_war))
            p.evac(psq.name, te)
            ev_toks.append(te)
        q_dma_toks.append(p.dma("gpsimd", lambda e, m=m, strip=strip: e.dma_start(
            out=q_o[:, m * T:(m + 1) * T], in_=strip[:, :]), dOUT, ev_toks))

    kv_grp_tok = None
    for ti in range(NT):
        k_ev = []
        v_ev = []
        for n in range(2):
            tmk = p.mm(ps_kA[n][:, :],
                       lambda k, ti=ti: xnf(k, ti * 128, (ti + 1) * 128),
                       lambda k, n=n: wkv_sb[:, k, n * 512:(n + 1) * 512],
                       8, deps=(t_xn, t_wkv[n], kv_grp_tok))
            te = elu1_evac(ps_kA[n][:, :], k_t[:, n * 512:(n + 1) * 512], n,
                           (tmk, kv_grp_tok))
            p.evac(ps_kA[n].name, te)
            k_ev.append(te)
            tmv = p.mm(ps_kB[n][:, :],
                       lambda k, ti=ti: xnf(k, ti * 128, (ti + 1) * 128),
                       lambda k, n=n: wkv_sb[:, k, 1024 + n * 512:1024 + (n + 1) * 512],
                       8, deps=(t_xn, t_wkv[2 + n], kv_grp_tok))
            tev = p.op("scalar", lambda e, n=n: e.copy(
                v_t[:, n * 512:(n + 1) * 512], ps_kB[n][:, :]), (tmv, kv_grp_tok))
            p.evac(ps_kB[n].name, tev)
            v_ev.append(tev)

        def kv_fn(e, ti=ti):
            last = None
            for h in range(H):
                j, base = h // 2, 64 * (h % 2)
                last = e.matmul(ps_kv[base:base + 64, 64 * j:64 * (j + 1)],
                                k_t[:, h * 64:(h + 1) * 64], v_t[:, h * 64:(h + 1) * 64],
                                start=(ti == 0 and h == 0), stop=(ti == NT - 1 and h == H - 1))
                last = e.matmul(ps_ks[base:base + 64, j:j + 1],
                                k_t[:, h * 64:(h + 1) * 64], ones_bf[:, :],
                                start=(ti == 0 and h == 0), stop=(ti == NT - 1 and h == H - 1))
            return last
        kv_grp_tok = p.op("tensor", kv_fn, tuple(k_ev + v_ev) + ((t_ones,) if ti == 0 else ()))

    tkv1 = p.op("vector", lambda e: e.tensor_copy(kvs[:, 0:512], ps_kv[:, :]), (kv_grp_tok,))
    tkv2 = p.op("vector", lambda e: e.tensor_copy(kvs[:, 512:520], ps_ks[:, 0:8]), (kv_grp_tok,))
    t_kv_o = p.dma("gpsimd", lambda e: e.dma_start(out=kv_o, in_=kvs[:, :]), dOUT, (tkv1, tkv2))

    def xn_src(i):
        sc = u_scr[i % 2]
        tn = p.op("vector", lambda e, i=i, sc=sc: e.tensor_scalar(
            sc[:, :], xt(i), mean[:, i:i + 1], rstd[:, i:i + 1],
            ALU.subtract, ALU.mult), t_ln1 + xn_src.war.get(i % 2, ()))
        return sc, tn

    xn_src.war = {}
    ln2_tile_toks = []
    for i in range(NT):
        sc, tn = xn_src(i)
        ta = p.op("vector", lambda e, i=i, sc=sc: e.reduce_sum(
            s1[:, i:i + 1], sc[:, :], axis=AX.X), (tn,))
        tb = p.op("scalar", lambda e, i=i, sc=sc: e.activation(
            dump[:, :], sc[:, :], AF.Square, accum_out=s2[:, i:i + 1]), (tn,))
        xn_src.war[i % 2] = (ta, tb)
        ln2_tile_toks += [ta, tb]
    t1 = p.op("vector", lambda e: e.tensor_scalar_mul(mean2[:, :], s1[:, :], 1.0 / D), ln2_tile_toks)
    t2 = p.op("vector", lambda e: e.tensor_scalar_mul(vscr[:, :], s2[:, :], 1.0 / D), ln2_tile_toks)
    t3 = p.op("vector", lambda e: e.tensor_mul(s1[:, :], mean2[:, :], mean2[:, :]), (t1,), flush=True)
    t4_ = p.op("vector", lambda e: e.tensor_sub(vscr[:, :], vscr[:, :], s1[:, :]), (t2, t3), flush=True)
    t5 = p.op("vector", lambda e: e.tensor_scalar_add(vscr[:, :], vscr[:, :], EPS), (t4_,), flush=True)
    t6 = p.op("scalar", lambda e: e.sqrt(s2[:, :], vscr[:, :]), (t5,))
    t_ln2 = p.op("vector", lambda e: e.reciprocal(rstd2[:, :], s2[:, :]), (t6,), flush=True)

    mx_toks = []
    xn2_toks = []
    for i in range(NT):
        sc = u_scr[i % 2]
        war = (xn2_toks[-2],) if len(xn2_toks) >= 2 else ()
        ta = p.op("vector", lambda e, i=i, sc=sc: e.tensor_scalar(
            sc[:, :], xt(i), mean[:, i:i + 1], rstd[:, i:i + 1],
            ALU.subtract, ALU.mult), (t_ln2, t_x) + war)
        tb = p.op("vector", lambda e, i=i, sc=sc: e.tensor_scalar(
            sc[:, :], sc[:, :], mean2[:, i:i + 1], rstd2[:, i:i + 1],
            ALU.subtract, ALU.mult), (ta,))
        tc = p.op("gpsimd", lambda e, sc=sc: e.tensor_mul(sc[:, :], sc[:, :], gam_sb[:, :]),
                  (tb, t_gam))
        td = p.op("gpsimd", lambda e, i=i, sc=sc: e.tensor_add(xt(i), sc[:, :], bet_sb[:, :]),
                  (tc, t_bet))
        xn2_toks.append(td)
        mx_toks.append(p.op("vector", lambda e, i=i: e.reduce_max(
            mxc[:, i:i + 1], xt(i), axis=AX.X, apply_absolute_value=True), (td,)))

    tmx = p.op("vector", lambda e: e.reduce_max(red[:, :], mxc[:, :], axis=AX.X,
                                                apply_absolute_value=True), mx_toks)
    tmxf = p.fence("vector", (tmx,))
    tb1 = p.dma("gpsimd", lambda e: e.dma_start(
        out=bounce.rearrange("one p -> p one"), in_=red[:, :]), dBN, (tmxf,))
    tb2 = p.dma("gpsimd", lambda e: e.dma_start(out=redT[:, :], in_=bounce), dBN, (tb1,))
    tgm = p.op("vector", lambda e: e.reduce_max(gmx[:, :], redT[:, :], axis=AX.X), (tb2,), flush=True)
    tgf = p.fence("vector", (tgm,))
    p.dma("gpsimd", lambda e: e.dma_start(out=mx_o, in_=gmx[:, :]), dOUT, (tgf,))
    p.dma("gpsimd", lambda e: e.dma_start(out=xn2_o, in_=x_tm[:, :]), dOUT, xn2_toks)

    return p.build()



def build_launch2(dbg=None, stop=None):
    p = Prog()
    nc = p.nc
    x_d = nc.dram_tensor("x", [T, D], F32, kind="ExternalInput").ap()
    q_d = nc.dram_tensor("q_fm", [128, 8 * T], BF16, kind="ExternalInput").ap()
    xn2_d = nc.dram_tensor("xn2", [128, NT * 1024], F32, kind="ExternalInput").ap()
    kvs_d = nc.dram_tensor("kv_self", [128, 520], F32, kind="ExternalInput").ap()
    kvp_d = nc.dram_tensor("kv_peer", [128, 520], F32, kind="ExternalInput").ap()
    gmx_d = nc.dram_tensor("gmaxv", [1, 8], F32, kind="ExternalInput").ap()
    wo_d = nc.dram_tensor("woT_slab", [2, 1024, 512], F32, kind="ExternalInput").ap()
    bo_d = nc.dram_tensor("b_out_row", [1, 1024], F32, kind="ExternalInput").ap()
    wffn_d = nc.dram_tensor("wffnT", [1024, 1024], F32, kind="ExternalInput").ap()
    w1_d = nc.dram_tensor("w1_slab", [6, 1024, 512], F32, kind="ExternalInput").ap()
    b1_d = nc.dram_tensor("b1_fm", [128, 24], F32, kind="ExternalInput").ap()
    rw_d = nc.dram_tensor("rwT", [1024, 16], F32, kind="ExternalInput").ap()
    cb_d = nc.dram_tensor("cb_b", [128, 16], F32, kind="ExternalInput").ap()
    guw_d = nc.dram_tensor("guw_slab", [16, 2, 2, 1024, 128], F32, kind="ExternalInput").ap()
    dw_d = nc.dram_tensor("dw_slab", [16, 2, 256, 512], F32, kind="ExternalInput").ap()
    sg_d = nc.dram_tensor("sgT_slab", [16, 1024, 128], F32, kind="ExternalInput").ap()
    su_d = nc.dram_tensor("suT_slab", [16, 1024, 128], F32, kind="ExternalInput").ap()
    sd_d = nc.dram_tensor("sdT_slab", [2, 2048, 512], F32, kind="ExternalInput").ap()
    oh_d = nc.dram_tensor("onehot", [16, 2048], F32, kind="ExternalInput").ap()
    sel_d = nc.dram_tensor("selhead", [8, 16, 128], F32, kind="ExternalInput").ap()
    id_d = nc.dram_tensor("ident", [128, 128], F32, kind="ExternalInput").ap()

    xp_o = nc.dram_tensor("xp", [128, NT * 1024], F32, kind="ExternalOutput").ap()
    hs_o = nc.dram_tensor("hso", [128, 8 * T], F32, kind="ExternalOutput").ap()
    as_o = nc.dram_tensor("aso", [128, 8 * T], F32, kind="ExternalOutput").ap()
    og_o = nc.dram_tensor("ogo", [128, 8 * T], F32, kind="ExternalOutput").ap()
    hf_o = nc.dram_tensor("hfin", [128, 8], F32, kind="ExternalOutput").ap()
    dbgs_o = nc.dram_tensor("dbgs", [2, 8], F32, kind="ExternalOutput").ap()
    dbgx_o = nc.dram_tensor("dbgx", [128, NT * 1024], F32, kind="ExternalOutput").ap()
    dbgb_o = nc.dram_tensor("dbgb", [128, 8 * T], BF16, kind="ExternalOutput").ap()
    dbgz_o = nc.dram_tensor("dbgz", [16, T], F32, kind="ExternalOutput").ap()
    dbgt_o = nc.dram_tensor("dbgt", [128, 8192], BF16, kind="ExternalOutput").ap()
    dbgc_o = nc.dram_tensor("dbgc", [128, 4], F32, kind="ExternalOutput").ap()
    bounce2 = nc.dram_tensor("bounce2", [3, 128], F32).ap()

    def _fin(p_, deps, sc11_, wstat_, x_tm_, dbgs_o_, dbgx_o_, dOUT_):
        p_.dma("gpsimd", lambda e: e.dma_start(out=dbgs_o_[1:2, :], in_=sc11_[:, :]), dOUT_, deps)
        p_.dma("gpsimd", lambda e: e.dma_start(out=dbgs_o_[0:1, :], in_=wstat_[:, :]), dOUT_, deps)
        p_.dma("gpsimd", lambda e: e.dma_start(out=dbgx_o_, in_=x_tm_[:, :]), dOUT_, deps)
        return p_.build()

    x_tm = p.sb("x_tm", [128, NT * 1024], F32)
    AR1 = p.sb("AR1", [128, 12288], F32)
    AR2 = p.sb("AR2", [128, 8 * T], BF16, is_transpose_dest=True)
    AR3 = p.sb("AR3", [128, 12288], BF16)
    kvsum = AR1[:, 2048:2568]
    kv_bd = AR1[:, 2568:3080].bitcast(BF16)
    zb_sb = [AR1[:, 1024 + j * 512:1024 + (j + 1) * 512] for j in range(2)]
    sel_sb = AR1[0:16, 2048 * 0 + 4096:4096 + 1024]
    xi_scr = AR1[:, 0:1024].bitcast(I32)
    oh_r = AR1[0:16, 8192:9216].bitcast(BF16)
    us_scr = [AR1[:, 9216 + j * 256:9216 + (j + 1) * 256].bitcast(BF16) for j in range(2)]
    ig_scr = [AR3[:, j * 512:(j + 1) * 512].bitcast(F32) for j in range(2)]
    xn1_scr = None
    zseg = AR3[:, 1024:1536].bitcast(F32)
    idb_sb = AR3[:, 1536:1664]
    s1 = p.sb("s1", [128, NT], F32)
    s2 = p.sb("s2", [128, NT], F32)
    mean = p.sb("mean", [128, NT], F32)
    rstd = p.sb("rstd", [128, NT], F32)
    vscr = p.sb("vscr", [128, NT], F32)
    dump = p.sb("dump", [128, 1024], F32)
    cbw_scr = [dump[:, j * 512:(j + 1) * 512] for j in range(2)]
    bd = p.sb("bd", [128, 128], BF16)
    ones_row = p.sb("ones_row", [1, 128], F32)
    ones_rbf = p.sb("ones_rbf", [1, 128], BF16)
    bo_bf = p.sb("bo_bf", [1, 1024], BF16)
    gval = p.sb("gval", [1, 8], F32)
    sc11 = p.sb("sc11", [1, 8], F32)
    inv_b = p.sb("inv_b", [128, 1], F32)
    cw_b = p.sb("cw_b", [128, 1], F32)
    del_b = p.sb("del_b", [128, 1], F32)
    wp_scr = p.sb("wp_scr", [128, 1024], F32)
    xb_scr = [p.sb(f"xb_scr{j}", [128, 1024], BF16) for j in range(2)]
    xn1_scr = xb_scr
    sA = p.sb("sA", [128, 8], F32)
    sC = p.sb("sC", [128, 2], F32)
    red3 = p.sb("red3", [1, 128], F32)
    red3b = p.sb("red3b", [1, 128], F32)
    red3c = p.sb("red3c", [1, 128], F32)
    wstat = p.sb("wstat", [1, 8], F32)
    snap = p.sb("snap", [1, 16], F32)
    cnt_cols = p.sb("cnt_cols", [128, 16], F32)
    b1_sb = p.sb("b1_sb", [128, 24], F32)
    cb_sb = p.sb("cb_sb", [128, 16], F32)
    rw_bf = p.sb("rw_bf", [128, 128], BF16)
    scores_fm = p.sb("scores_fm", [16, T], F32)
    z_sb = scores_fm
    sc_tm = p.sb("sc_tm", [128, 256], F32)
    sbt = p.sb("sbt", [128, 256], F32)
    pairs = p.sb("pairs", [128, 384], F32)
    gsc = p.sb("gsc", [128, 64], F32)
    mins = p.sb("mins", [128, 96], F32)
    thr2 = p.sb("thr2", [128, 16], F32)
    ni_sb = p.sb("ni_sb", [128, 64], F32)
    masked = p.sb("masked", [128, 256], F32)
    selm = p.sb("selm", [128, 256], F32)
    mx8 = sbt[:, 0:128]
    wun = p.sb("wun", [128, 256], F32)
    dnm = p.sb("dnm", [128, 16], F32)
    comb = p.sb("comb", [128, 256], F32)
    C_fm = p.sb("C_fm", [16, T], BF16)
    id_sb = p.sb("id_sb", [128, 128], F32)
    carry_h = p.sb("carry_h", [128, 8], F32)
    carry_a = p.sb("carry_a", [128, 8], F32)

    ps = [p.ps(f"ps{j}") for j in range(8)]

    dIN = p.dma_sem("dIN")
    dIN2 = p.dma_sem("dIN2")
    dTR = p.dma_sem("dTR")
    dOUT = p.dma_sem("dOUT")
    dOUT2 = p.dma_sem("dOUT2")
    dBN = p.dma_sem("dBN")
    dW = [p.dma_sem(f"dW{j}") for j in range(4)]
    dXQ = p.dma_sem("dXQ")

    def xs_(i, a=0, b=1024):
        return x_tm[:, i * 1024 + a:i * 1024 + b]

    def ar2(c, a, b):
        return AR2[:, c * T + a:c * T + b]

    t_x = p.dma("sync", lambda e: e.dma_start(
        out=x_tm[:, :].rearrange("p (i f) -> p i f", i=NT),
        in_=x_d.rearrange("(i p) f -> p i f", p=128)), dIN2)
    t_q = p.dma("sync", lambda e: e.dma_start(out=AR2[:, :], in_=q_d), dIN2)
    t_kvs = p.dma("sync", lambda e: e.dma_start(out=kvsum, in_=kvs_d), dIN2)
    t_kvp = p.dma("sync", lambda e: e.dma_start(out=dump[:, 0:520], in_=kvp_d), dIN2)
    t_gmx = p.dma("sync", lambda e: e.dma_start(out=gval[:, :], in_=gmx_d), dIN2)
    t_sel = p.dma("sync", lambda e: e.dma_start(
        out=sel_sb.rearrange("q (c p) -> q c p", c=8),
        in_=sel_d.transpose([1, 0, 2])), dIN2)
    t_id = p.dma("sync", lambda e: e.dma_start(out=id_sb[:, :], in_=id_d), dIN2)
    t_b1 = p.dma("sync", lambda e: e.dma_start(out=b1_sb[:, :], in_=b1_d), dIN2)
    t_cb = p.dma("sync", lambda e: e.dma_start(out=cb_sb[:, :], in_=cb_d), dIN2)
    wo_v = AR3[:, 0:8192].rearrange("p (k c) -> p k c", k=8)
    t_wo = p.dma("gpsimd", lambda e: e.dma_start(
        out=wo_v[:, :, 0:512], in_=wo_d[0].rearrange("(k p) c -> p k c", p=128)), dIN)
    t_wo2 = p.dma("gpsimd", lambda e: e.dma_start(
        out=wo_v[:, :, 512:1024], in_=wo_d[1].rearrange("(k p) c -> p k c", p=128)), dIN)
    t_bo = p.dma("gpsimd", lambda e: e.dma_start(out=bo_bf[:, :], in_=bo_d), dIN)
    t_rw = p.dma("gpsimd", lambda e: e.dma_start(
        out=rw_bf[:, :].rearrange("p (k c) -> p k c", k=8),
        in_=rw_d.rearrange("(k p) c -> p k c", p=128)), dIN)
    t_ones = p.op("vector", lambda e: e.memset(ones_row[:, :], 1.0))
    t_onesb = p.op("vector", lambda e: e.memset(ones_rbf[:, :], 1.0))

    tg1 = p.op("vector", lambda e: e.reduce_max(sc11[:, 0:1], gval[:, :], axis=AX.X), (t_gmx,))
    tg2 = p.op("vector", lambda e: e.tensor_scalar_max(sc11[:, 0:1], sc11[:, 0:1], 1e-8), (tg1,), flush=True)
    tg3 = p.op("vector", lambda e: e.tensor_scalar_mul(sc11[:, 1:2], sc11[:, 0:1], 1.0 / 127.0), (tg2,), flush=True)
    tg4 = p.op("vector", lambda e: e.reciprocal(sc11[:, 2:3], sc11[:, 1:2]), (tg3,), flush=True)
    tg5 = p.mm(ps[4][:, 0:1], lambda k: ones_row[:, :], lambda k: sc11[:, 2:3], 1,
               deps=(tg4, t_ones))
    tg6 = p.op("vector", lambda e: e.tensor_copy(inv_b[:, :], ps[4][:, 0:1]), (tg5,))
    p.evac(ps[4].name, tg6)

    tk1 = p.op("vector", lambda e: e.tensor_add(kvsum, kvsum, dump[:, 0:520]),
               (t_kvs, t_kvp))
    tk2a = p.op("vector", lambda e: e.memset(kv_bd, 0.0), (tk1,), flush=True)
    tk2_l = []
    for j in range(8):
        tk2_l.append(p.op("vector", lambda e, j=j: e.tensor_copy(
            kv_bd[0:64, 128 * j:128 * j + 64], kvsum[0:64, 64 * j:64 * (j + 1)]),
            (tk1, tk2a)))
        tk2_l.append(p.op("vector", lambda e, j=j: e.tensor_copy(
            kv_bd[64:128, 128 * j + 64:128 * (j + 1)], kvsum[64:128, 64 * j:64 * (j + 1)]),
            (tk1, tk2a)))
    tk2 = tk2_l[-1]
    all_tk2 = tuple(tk2_l)
    tbd0 = p.op("vector", lambda e: e.memset(bd[:, :], 0.0))
    tbd = [tbd0]
    for c in range(8):
        tbd.append(p.op("vector", lambda e, c=c: e.tensor_copy(
            bd[0:64, c * 16 + 2 * c:c * 16 + 2 * c + 1],
            kvsum[0:64, 512 + c:513 + c]), (tbd0, tk1)))
        tbd.append(p.op("vector", lambda e, c=c: e.tensor_copy(
            bd[64:128, c * 16 + 2 * c + 1:c * 16 + 2 * c + 2],
            kvsum[64:128, 512 + c:513 + c]), (tbd0, tk1)))

    if stop == "tg":
        return _fin(p, (tg6, tk2) + tuple(tbd), sc11, wstat, x_tm, dbgs_o, dbgx_o, dOUT)
    den_toks = []
    for t4 in range(4):
        tm = p.mm(ps[5][0:16, :], lambda k: bd[:, k * 16:(k + 1) * 16],
                  lambda k, t4=t4: ar2(k, t4 * 512, (t4 + 1) * 512), 8,
                  deps=tuple(tbd) + (t_q,))
        td1 = p.op("vector", lambda e, t4=t4: e.tensor_scalar_add(
            z_sb[:, t4 * 512:(t4 + 1) * 512], ps[5][0:16, :], 1e-6), (tm,))
        td2 = p.op("vector", lambda e, t4=t4: e.reciprocal(
            z_sb[:, t4 * 512:(t4 + 1) * 512], z_sb[:, t4 * 512:(t4 + 1) * 512]), (td1,))
        p.evac(ps[5].name, td2)
        den_toks.append(td2)
    all_den = tuple(den_toks)

    attn_toks = []
    zb_war = {}
    for c in range(8):
        for t4 in range(4):
            tz = p.mm(ps[4][:, :], lambda k, c=c: sel_sb[:, c * 128:(c + 1) * 128],
                      lambda k, t4=t4: z_sb[:, t4 * 512:(t4 + 1) * 512], 1,
                      deps=(den_toks[t4], t_sel))
            tze = p.op("scalar", lambda e, t4=t4: e.copy(zb_sb[t4 % 2], ps[4][:, :]),
                       (tz, zb_war.get(t4 % 2)))
            p.evac(ps[4].name, tze)

            def qkv_fn(e, c=c, t4=t4):
                return e.matmul(ps[6][:, :], kv_bd[:, c * 128:(c + 1) * 128],
                                AR2[:, c * T + t4 * 512:c * T + (t4 + 1) * 512],
                                start=True, stop=True)
            depsq = all_tk2 + (t_q,) + all_den
            if ps[6].name in p.bank_tok:
                depsq += (p.bank_tok[ps[6].name],)
            tqk = p.op("tensor", qkv_fn, depsq)
            ta = p.op("vector", lambda e, c=c, t4=t4: e.tensor_tensor(
                ar2(c, t4 * 512, (t4 + 1) * 512), ps[6][:, :], zb_sb[t4 % 2],
                ALU.mult), (tqk, tze) + all_den)
            p.evac(ps[6].name, ta)
            zb_war[t4 % 2] = ta
            attn_toks.append(ta)
    all_attn = tuple(attn_toks)

    if stop == "C":
        p.dma("gpsimd", lambda e: e.dma_start(out=dbgb_o, in_=AR2[:, :]), dOUT, all_attn)
        p.dma("gpsimd", lambda e: e.dma_start(out=dbgz_o, in_=z_sb[:, :]), dOUT, all_den)
        return _fin(p, all_attn + (tg6,), sc11, wstat, x_tm, dbgs_o, dbgx_o, dOUT)
    d_toks = []
    for ti in range(NT):
        for n2 in range(2):
            psd = ps[n2]
            tmo = p.mm(psd[:, :],
                       lambda k, ti=ti: AR2[:, k * T + ti * 128:k * T + (ti + 1) * 128],
                       lambda k, n2=n2: wo_v[:, k, n2 * 512:(n2 + 1) * 512], 8,
                       deps=all_attn + (t_wo, t_wo2, t_bo, t_onesb), stop=False,
                       extras=(lambda e, n2=n2, psd=psd: e.matmul(
                           psd[:, :], ones_rbf[:, :], bo_bf[:, n2 * 512:(n2 + 1) * 512],
                           start=False, stop=True),))
            tev = p.op("vector", lambda e, ti=ti, n2=n2, psd=psd: e.scalar_tensor_tensor(
                xs_(ti, n2 * 512, (n2 + 1) * 512), psd[:, :], 0.0,
                xs_(ti, n2 * 512, (n2 + 1) * 512), ALU.add, ALU.add), (tmo, t_x))
            p.evac(psd.name, tev)
            d_toks.append(tev)
    all_d = tuple(d_toks)

    if stop == "D":
        return _fin(p, all_d + (tg6,), sc11, wstat, x_tm, dbgs_o, dbgx_o, dOUT)
    strips = AR3[:, 8192:12288].bitcast(F32)
    t_bf_v = AR3[:, 0:8192]

    p1_toks = []
    for k in range(8):
        sl = strips[:, (k % 2) * 1024:(k % 2) * 1024 + 1024]
        war = (p1_toks[-2],) if len(p1_toks) >= 2 else ()
        tl = p.dma("gpsimd", lambda e, k=k, sl=sl: e.dma_start(
            out=sl, in_=wffn_d[k * 128:(k + 1) * 128, :]), dW[0], war)
        tr = p.op("vector", lambda e, k=k, sl=sl: e.reduce_sum(
            sA[:, k:k + 1], sl, axis=AX.X, apply_absolute_value=True), (tl,))
        p1_toks.append(tr)
    tps2 = p.op("vector", lambda e: e.reduce_sum(sC[:, 0:1], sA[:, :], axis=AX.X), tuple(p1_toks), flush=True)
    tpsf = p.fence("vector", (tps2,))
    tb1 = p.dma("gpsimd", lambda e: e.dma_start(
        out=bounce2[0:1, :].rearrange("one p -> p one"), in_=sC[:, 0:1]), dBN, (tpsf,))
    tb2 = p.dma("gpsimd", lambda e: e.dma_start(out=red3[:, :], in_=bounce2[0:1, :]), dBN, (tb1,))
    tds = p.op("vector", lambda e: e.reduce_sum(wstat[:, 0:1], red3[:, :], axis=AX.X), (tb2,), flush=True)
    tdd = p.op("vector", lambda e: e.tensor_scalar_mul(
        wstat[:, 7:8], wstat[:, 0:1], 0.7 / (1024.0 * 1024.0)), (tds,), flush=True)
    tdb = p.mm(ps[4][:, 0:1], lambda k: ones_row[:, :], lambda k: wstat[:, 7:8], 1,
               deps=(tdd, t_ones))
    tdbe = p.op("vector", lambda e: e.tensor_copy(del_b[:, :], ps[4][:, 0:1]), (tdb,))
    p.evac(ps[4].name, tdbe)

    p2_last = []
    strip_war2 = []
    for k in range(8):
        sl = strips[:, (k % 2) * 1024:(k % 2) * 1024 + 1024]
        war = (strip_war2[-2],) if len(strip_war2) >= 2 else (p1_toks[-1],)
        tl = p.dma("gpsimd", lambda e, k=k, sl=sl: e.dma_start(
            out=sl, in_=wffn_d[k * 128:(k + 1) * 128, :]), dW[0], (tdbe,) + war)
        tabs = p.op("scalar", lambda e, sl=sl: e.activation(dump[:, :], sl, AF.Abs),
                    (tl,) + (((strip_war2[-2],) if len(strip_war2) >= 2 else ())))
        tmk0 = p.op("vector", lambda e, k=k: e.tensor_single_scalar(
            dump[:, :], dump[:, :], del_b[:, 0:1], ALU.is_gt), (tabs, tdbe))
        tmk = p.op("vector", lambda e, k=k: e.reduce_sum(
            cnt_cols[:, k:k + 1], dump[:, :], axis=AX.X), (tmk0,))
        tam = p.op("vector", lambda e, sl=sl: e.scalar_tensor_tensor(
            wp_scr[:, :], sl, 0.0, dump[:, :], ALU.bypass, ALU.mult), (tmk,))
        tamr = p.op("vector", lambda e, k=k: e.reduce_sum(
            cnt_cols[:, 8 + k:9 + k], wp_scr[:, :], axis=AX.X,
            apply_absolute_value=True), (tam,))
        tt = p.op("scalar", lambda e, k=k: e.sign(
            t_bf_v[:, k * 1024:(k + 1) * 1024], wp_scr[:, :]),
            (tam, tamr) + all_d)
        strip_war2.append(tt)
        p2_last.append(tt)
    tc1 = p.op("vector", lambda e: e.reduce_sum(sC[:, 0:1], cnt_cols[:, 0:8], axis=AX.X), tuple(p2_last) + (tb1,))
    tc2 = p.op("vector", lambda e: e.reduce_sum(sC[:, 1:2], cnt_cols[:, 8:16], axis=AX.X), tuple(p2_last))
    tcf = p.fence("vector", (tc1, tc2))
    tb3a = p.dma("gpsimd", lambda e: e.dma_start(
        out=bounce2[1:2, :].rearrange("one p -> p one"), in_=sC[:, 0:1]), dBN, (tcf,))
    tb3 = p.dma("gpsimd", lambda e: e.dma_start(
        out=bounce2[2:3, :].rearrange("one p -> p one"), in_=sC[:, 1:2]), dBN, (tc2, tb3a))
    tb4 = p.dma("gpsimd", lambda e: e.dma_start(out=red3b[:, :], in_=bounce2[1:2, :]), dBN, (tb3,))
    tb4b = p.dma("gpsimd", lambda e: e.dma_start(out=red3c[:, :], in_=bounce2[2:3, :]), dBN, (tb4,))
    ta1 = p.op("vector", lambda e: e.reduce_sum(wstat[:, 1:2], red3b[:, :], axis=AX.X), (tb4,), flush=True)
    ta2 = p.op("vector", lambda e: e.reduce_sum(wstat[:, 2:3], red3c[:, :], axis=AX.X), (tb4b,), flush=True)
    ta3 = p.op("vector", lambda e: e.tensor_scalar_max(wstat[:, 1:2], wstat[:, 1:2], 1.0), (ta1,), flush=True)
    ta4 = p.op("vector", lambda e: e.reciprocal(wstat[:, 3:4], wstat[:, 1:2]), (ta3,), flush=True)
    ta5 = p.op("vector", lambda e: e.tensor_mul(wstat[:, 4:5], wstat[:, 2:3], wstat[:, 3:4]), (ta2, ta4), flush=True)
    ta6 = p.op("vector", lambda e: e.tensor_mul(wstat[:, 5:6], wstat[:, 4:5], sc11[:, 1:2]), (ta5, tg3), flush=True)
    ta7 = p.mm(ps[4][:, 0:1], lambda k: ones_row[:, :], lambda k: wstat[:, 5:6], 1,
               deps=(ta6, t_ones))
    ta8 = p.op("vector", lambda e: e.tensor_copy(cw_b[:, :], ps[4][:, 0:1]), (ta7,))
    p.evac(ps[4].name, ta8)

    tsnap1 = p.op("vector", lambda e: e.tensor_copy(snap[:, 0:3], sc11[:, 0:3]), (tg6,))
    tsnap2 = p.op("vector", lambda e: e.tensor_copy(snap[:, 3:4], inv_b[0:1, 0:1]), (tg6,))
    xq_toks = []
    qt_bufs = [wp_scr, dump]
    for i in range(NT):
        sl = qt_bufs[i % 2]
        slb = xb_scr[i % 2]
        war = (xq_toks[-2],) if len(xq_toks) >= 2 else tuple(p2_last) + (tc1, tc2)
        tl = p.dma("sync", lambda e, i=i, sl=sl: e.dma_start(
            out=sl[:, :], in_=xn2_d[:, i * 1024:(i + 1) * 1024]), dXQ, war)
        tq1 = p.op("vector", lambda e, sl=sl: e.tensor_scalar(
            sl[:, :], sl[:, :], inv_b[:, 0:1], 127.0, ALU.mult, ALU.min), (tl, tg6))
        tq2 = p.op("vector", lambda e, sl=sl: e.tensor_scalar_max(sl[:, :], sl[:, :], -127.0), (tq1,))
        tq3 = p.op("vector", lambda e, sl=sl: e.tensor_copy(xi_scr, sl[:, :]), (tq2,))
        tq4 = p.op("vector", lambda e, slb=slb: e.tensor_copy(slb[:, :], xi_scr), (tq3,))
        gate = all_attn + all_d if i < 2 else ()
        tq5 = p.dma("sync", lambda e, i=i, slb=slb: e.dma_start_transpose(
            AR2[:, :].rearrange("p (c t) -> p c t", c=8)[:, :, i * 128:(i + 1) * 128],
            slb[:, :]), dTR, (tq4,) + gate)
        xq_toks.append(tq5)
    all_xq = tuple(xq_toks)

    e_toks = []
    for ti in range(NT):
        for n2 in range(2):
            psd = ps[n2]
            tmf = p.mm(psd[:, :],
                       lambda k, ti=ti: AR2[:, k * T + ti * 128:k * T + (ti + 1) * 128],
                       lambda k, n2=n2: t_bf_v[:, k * 1024 + n2 * 512:k * 1024 + (n2 + 1) * 512], 8,
                       deps=all_xq + tuple(p2_last))
            tev = p.op("vector", lambda e, ti=ti, n2=n2, psd=psd: e.scalar_tensor_tensor(
                xs_(ti, n2 * 512, (n2 + 1) * 512), psd[:, :], cw_b[:, 0:1],
                xs_(ti, n2 * 512, (n2 + 1) * 512), ALU.mult, ALU.add), (tmf, ta8))
            p.evac(psd.name, tev)
            e_toks.append(tev)
    all_e = tuple(e_toks)

    if stop == "E":
        p.dma("gpsimd", lambda e: e.dma_start(out=dbgc_o[:, 0:1], in_=cw_b[:, :], allow_slow_non_contiguous=True), dOUT, (ta8,) + all_e)
        p.dma("gpsimd", lambda e: e.dma_start(out=dbgc_o[:, 1:2], in_=inv_b[:, :], allow_slow_non_contiguous=True), dOUT, (tg6,) + all_e)
        p.dma("gpsimd", lambda e: e.dma_start(out=dbgb_o, in_=AR2[:, :]), dOUT, all_xq + all_e)
        p.dma("gpsimd", lambda e: e.dma_start(out=dbgt_o, in_=t_bf_v), dOUT, tuple(p2_last) + all_e)
        return _fin(p, all_e + (tg6, ta8), sc11, wstat, x_tm, dbgs_o, dbgx_o, dOUT)
    f_stat_toks = []
    for i in range(NT):
        dep_i = all_e + all_d
        f_stat_toks.append(p.op("vector", lambda e, i=i: e.reduce_sum(
            s1[:, i:i + 1], xs_(i), axis=AX.X), dep_i))
        f_stat_toks.append(p.op("scalar", lambda e, i=i: e.activation(
            dump[:, :], xs_(i), AF.Square, accum_out=s2[:, i:i + 1]),
            dep_i + tuple(p2_last) + (xq_toks[1], xq_toks[-1])))
    tf1 = p.op("vector", lambda e: e.tensor_scalar_mul(mean[:, :], s1[:, :], 1.0 / D), f_stat_toks)
    tf2 = p.op("vector", lambda e: e.tensor_scalar_mul(vscr[:, :], s2[:, :], 1.0 / D), f_stat_toks)
    tf3 = p.op("vector", lambda e: e.tensor_mul(s1[:, :], mean[:, :], mean[:, :]), (tf1,), flush=True)
    tf4 = p.op("vector", lambda e: e.tensor_sub(vscr[:, :], vscr[:, :], s1[:, :]), (tf2, tf3), flush=True)
    tf5 = p.op("vector", lambda e: e.tensor_scalar_add(vscr[:, :], vscr[:, :], EPS), (tf4,), flush=True)
    tf6 = p.op("scalar", lambda e: e.sqrt(s2[:, :], vscr[:, :]), (tf5,))
    tf7 = p.op("vector", lambda e: e.reciprocal(rstd[:, :], s2[:, :]), (tf6,), flush=True)

    xn1_tn = []
    xn1_toks = []
    for i in range(NT):
        slb = xn1_scr[i % 2]
        war = (xn1_toks[-2],) if len(xn1_toks) >= 2 else ()
        tn = p.op("vector", lambda e, i=i, slb=slb: e.tensor_scalar(
            slb[:, :], xs_(i), mean[:, i:i + 1], rstd[:, i:i + 1],
            ALU.subtract, ALU.mult), (tf7,) + war + ((all_xq[-2], all_xq[-1]) if i < 2 else ()))
        gate = tuple(e_toks) if i < 2 else ()
        tt = p.dma("sync", lambda e, i=i, slb=slb: e.dma_start_transpose(
            AR2[:, :].rearrange("p (c t) -> p c t", c=8)[:, :, i * 128:(i + 1) * 128],
            slb[:, :]), dTR, (tn,) + gate)
        xn1_tn.append(tn)
        xn1_toks.append(tt)
    t_xn1 = tuple(xn1_toks)
    xgate = xn1_tn[-1]

    if stop == "F":
        return _fin(p, t_xn1 + (tg6, ta8), sc11, wstat, x_tm, dbgs_o, dbgx_o, dOUT)
    sco_toks = []
    sctm_toks = []
    for ti in range(NT):
        tm = p.mm(ps[5][:, 0:16],
                  lambda k, ti=ti: AR2[:, k * T + ti * 128:k * T + (ti + 1) * 128],
                  lambda k: rw_bf[:, k * 16:(k + 1) * 16], 8,
                  deps=t_xn1 + (t_rw,))
        te = p.op("scalar", lambda e, ti=ti: e.activation(
            sc_tm[:, ti * 16:(ti + 1) * 16], ps[5][:, 0:16], AF.Sigmoid), (tm,))
        p.evac(ps[5].name, te)
        sctm_toks.append(te)
    tscf = p.fence("scalar", tuple(sctm_toks))
    tsc_early = p.dma("gpsimd", lambda e: e.dma_start(out=dbgt_o[:, 256:512], in_=sc_tm[:, :]), dBN, (tscf,))
    if stop == "H0":
        return _fin(p, tuple(sctm_toks) + (tscf,), sc11, wstat, x_tm, dbgs_o, dbgx_o, dOUT)
    sbt_toks = [p.op("vector", lambda e, ti=ti: e.tensor_add(
        sbt[:, ti * 16:(ti + 1) * 16], sc_tm[:, ti * 16:(ti + 1) * 16], cb_sb[:, :]),
        (sctm_toks[ti], t_cb), flush=(ti == 0)) for ti in range(NT)]

    def V(t):
        return t[:, :].rearrange("p (ti g i) -> p ti g i", g=4, i=4)
    PAIRS = [(0, 1), (0, 2), (0, 3), (1, 2), (1, 3), (2, 3)]
    pr_toks = [p.op("vector", lambda e, j=j, a=a, b=b: e.tensor_add(
        pairs[:, j * 64:(j + 1) * 64].rearrange("p (ti g) -> p ti g", g=4),
        V(sbt)[:, :, :, a], V(sbt)[:, :, :, b]), tuple(sbt_toks), flush=(j == 0))
        for j, (a, b) in enumerate(PAIRS)]
    tgs = p.op("vector", lambda e: e.tensor_max(gsc[:, :], pairs[:, 0:64], pairs[:, 64:128]), pr_toks[0:2], flush=True)
    for j in range(2, 6):
        tgs = p.op("vector", lambda e, j=j: e.tensor_max(
            gsc[:, :], gsc[:, :], pairs[:, j * 64:(j + 1) * 64]), (tgs, pr_toks[j]), flush=True)
    GV = gsc[:, :].rearrange("p (ti g) -> p ti g", g=4)
    mn_toks = [p.op("vector", lambda e, j=j, a=a, b=b: e.tensor_tensor(
        mins[:, j * 16:(j + 1) * 16], GV[:, :, a], GV[:, :, b], ALU.min), (tgs,), flush=(j == 0))
        for j, (a, b) in enumerate(PAIRS)]
    tt2 = p.op("vector", lambda e: e.tensor_max(thr2[:, :], mins[:, 0:16], mins[:, 16:32]), mn_toks[0:2], flush=True)
    for j in range(2, 6):
        tt2 = p.op("vector", lambda e, j=j: e.tensor_max(
            thr2[:, :], thr2[:, :], mins[:, j * 16:(j + 1) * 16]), (tt2, mn_toks[j]), flush=True)
    ni_toks = [p.op("vector", lambda e, g=g: e.tensor_tensor(
        ni_sb[:, g * 16:(g + 1) * 16], GV[:, :, g], thr2[:, :], ALU.is_lt), (tt2,), flush=(g == 0))
        for g in range(4)]
    msk_toks = []
    for g in range(4):
        for i_ in range(4):
            msk_toks.append(p.op("vector", lambda e, g=g, i_=i_: e.scalar_tensor_tensor(
                V(masked)[:, :, g, i_], ni_sb[:, g * 16:(g + 1) * 16], -1e30,
                V(sbt)[:, :, g, i_], ALU.mult, ALU.add), (ni_toks[g],), flush=(g == 0 and i_ == 0)))
    mxt_toks = [p.op("vector", lambda e, ti=ti: e.max(
        mx8[:, ti * 8:ti * 8 + 8], masked[:, ti * 16:(ti + 1) * 16]), tuple(msk_toks), flush=(ti == 0))
        for ti in range(NT)]
    sel_toks = [p.op("vector", lambda e, ti=ti: e.tensor_single_scalar(
        selm[:, ti * 16:(ti + 1) * 16], masked[:, ti * 16:(ti + 1) * 16],
        mx8[:, ti * 8 + 3:ti * 8 + 4], ALU.is_ge), (mxt_toks[ti],), flush=(ti == 0)) for ti in range(NT)]
    twu = p.op("vector", lambda e: e.tensor_mul(wun[:, :], sc_tm[:, :], selm[:, :]),
               tuple(sel_toks) + tuple(sctm_toks), flush=True)
    tdn = p.op("vector", lambda e: e.reduce_sum(
        dnm[:, :], wun[:, :].rearrange("p (ti q) -> p ti q", q=16), axis=AX.X), (twu,), flush=True)
    tdn2 = p.op("vector", lambda e: e.tensor_scalar_add(dnm[:, :], dnm[:, :], 1e-20), (tdn,), flush=True)
    tdn3 = p.op("vector", lambda e: e.reciprocal(dnm[:, :], dnm[:, :]), (tdn2,), flush=True)
    cmb_toks = [p.op("vector", lambda e, ti=ti: e.tensor_single_scalar(
        comb[:, ti * 16:(ti + 1) * 16], wun[:, ti * 16:(ti + 1) * 16],
        dnm[:, ti:ti + 1], ALU.mult), (tdn3,), flush=(ti == 0)) for ti in range(NT)]
    comb_bf = masked[:, :].bitcast(BF16)[:, 0:256]
    tcb_bf = p.op("vector", lambda e: e.tensor_copy(comb_bf, comb[:, :]),
                  tuple(cmb_toks) + tuple(sel_toks) + tuple(mxt_toks), flush=True)
    tid_bf = p.op("vector", lambda e: e.tensor_copy(idb_sb[:, :], id_sb[:, :]), (t_id,))
    cfm_toks = []
    for ti in range(NT):
        bk = (p.bank_tok.get(ps[7].name),)
        tt = p.op("tensor", lambda e, ti=ti: e.transpose(
            ps[7][:, :].bitcast(BF16)[0:16, 0:128], comb_bf[:, ti * 16:(ti + 1) * 16],
            idb_sb[:, :]), (tcb_bf, tid_bf) + bk)
        te = p.op("vector", lambda e, ti=ti: e.tensor_copy(
            C_fm[:, ti * 128:(ti + 1) * 128], ps[7][:, :].bitcast(BF16)[0:16, 0:128]), (tt,))
        p.evac(ps[7].name, te)
        cfm_toks.append(te)
    all_cfm = tuple(cfm_toks)
    tsnap3 = p.op("vector", lambda e: e.tensor_copy(snap[:, 4:7], sc11[:, 0:3]), all_cfm)

    if stop == "H":
        tfh = p.fence("vector", tuple(cmb_toks) + tuple(sctm_toks) + (twu, tdn3))
        p.dma("gpsimd", lambda e: e.dma_start(out=dbgt_o[:, 0:256], in_=comb[:, :]), dBN, (tfh,))
        p.dma("gpsimd", lambda e: e.dma_start(out=dbgt_o[:, 512:768], in_=selm[:, :]), dBN, (tfh,))
        p.dma("gpsimd", lambda e: e.dma_start(out=dbgt_o[:, 768:832], in_=gsc[:, :]), dBN, (tfh,))
        p.dma("gpsimd", lambda e: e.dma_start(out=dbgt_o[:, 832:848], in_=thr2[:, :]), dBN, (tfh,))
        p.dma("gpsimd", lambda e: e.dma_start(out=dbgt_o[:, 1024:1280], in_=sbt[:, :]), dBN, (tfh,))
        p.dma("gpsimd", lambda e: e.dma_start(out=dbgt_o[:, 1280:1536], in_=wun[:, :]), dBN, (tfh,))
        p.dma("gpsimd", lambda e: e.dma_start(out=dbgz_o[:, 0:1024], in_=C_fm[:, :].bitcast(F32)), dBN, all_cfm)
        p.dma("gpsimd", lambda e: e.dma_start(out=dbgb_o, in_=AR2[:, :]), dBN, tuple(sctm_toks) + t_xn1)
        return _fin(p, all_cfm + (tg6, ta8), sc11, wstat, x_tm, dbgs_o, dbgx_o, dOUT)
    SEG = 256
    NSEG = T // SEG
    inp_v = AR1[:, 0:2048].rearrange("p (c t) -> p c t", c=8)
    f_v = AR1[:, 2048:4096].rearrange("p (c t) -> p c t", c=8)
    og_v = [AR1[:, 4096:6144].rearrange("p (c t) -> p c t", c=8) for j in range(2)]
    h_v = [AR1[:, 6144 + j * 2048:6144 + (j + 1) * 2048].rearrange("p (c t) -> p c t", c=8)
           for j in range(2)]
    a_buf = AR1[:, 10240:12288]
    tzs = p.op("vector", lambda e: e.memset(zseg, 0.0), all_e)
    tch = p.op("vector", lambda e: e.memset(carry_h[:, :], 0.0))
    tca = p.op("vector", lambda e: e.memset(carry_a[:, :], 1.0))

    wg_bufs = [AR3[:, 1664 + j * 4096:1664 + (j + 1) * 4096] for j in range(2)]
    grp_consumer = {}
    grp_tok = {}
    prev_scan_last = None
    h_war = [None, None]
    og_war = [None, None]
    a_war = None
    ig_war = {}
    g_final = []
    for s in range(NSEG):
        m_toks = []
        scan_deps_extra = (prev_scan_last,) if prev_scan_last else ()
        def load_grp(g6, s=s):
            wbg = wg_bufs[g6 % 2]
            war_w = (grp_consumer.get(g6 % 2),)
            gate_w = (tuple(p2_last) + (xq_toks[-1],)) if (s == 0 and g6 < 2) else ()
            grp_tok[(s, g6)] = p.dma("gpsimd", lambda e, g6=g6, wbg=wbg: e.dma_start(
                out=wbg.rearrange("p (k mc) -> p k mc", k=8),
                in_=w1_d[g6].rearrange("(k p) mc -> p k mc", p=128)),
                dW[1 + (g6 % 2)], war_w + gate_w)
        load_grp(0)
        load_grp(1)
        for m in range(24):
            g6 = m // 4
            wbg = wg_bufs[g6 % 2]
            tl = grp_tok[(s, g6)]
            psd = ps[2 + (m % 2)]
            tm = p.mm(psd[:, 0:SEG],
                      lambda k, wbg=wbg, m=m: wbg.rearrange(
                          "p (k mc) -> p k mc", k=8)[:, k, (m % 4) * 128:(m % 4 + 1) * 128],
                      lambda k, s=s: ar2(k, s * SEG, (s + 1) * SEG), 8,
                      deps=(tl,) + t_xn1)
            if m % 4 == 3:
                grp_consumer[g6 % 2] = tm
                if g6 + 2 < 6:
                    load_grp(g6 + 2)
            if m < 8:
                te = p.op("scalar", lambda e, m=m, psd=psd: e.activation(
                    inp_v[:, m, :], psd[:, 0:SEG], AF.Tanh, bias=b1_sb[:, m:m + 1]),
                    (tm, t_b1) + scan_deps_extra)
                rel = te
            elif m < 16:
                c = m - 8
                ig = ig_scr[c % 2]
                te0 = p.op("scalar", lambda e, m=m, ig=ig, psd=psd: e.activation(
                    ig, psd[:, 0:SEG], AF.Sigmoid, bias=b1_sb[:, m:m + 1]),
                    (tm, t_b1, ig_war.get(c % 2)) + scan_deps_extra + (all_e if (s == 0 and m < 10) else ()))
                te1 = p.op("vector", lambda e, c=c, ig=ig: e.tensor_scalar(
                    f_v[:, c, :], ig, -1.0, 1.0, ALU.mult, ALU.add),
                    (te0,) + scan_deps_extra)
                te = p.op("vector", lambda e, c=c, ig=ig: e.tensor_mul(
                    inp_v[:, c, :], ig, inp_v[:, c, :]), (te1, m_toks[c]))
                ig_war[c % 2] = te
                rel = te0
            else:
                c = m - 16
                te = p.op("scalar", lambda e, c=c, s=s, psd=psd, m=m: e.activation(
                    og_v[s % 2][:, c, :], psd[:, 0:SEG], AF.Sigmoid, bias=b1_sb[:, m:m + 1]),
                    (tm, t_b1, og_war[s % 2]))
                rel = te
            p.evac(psd.name, rel)
            m_toks.append(te)
        scan_toks = []
        for c in range(8):
            th = p.op("vector", lambda e, c=c, s=s: e.tensor_tensor_scan(
                h_v[s % 2][:, c, :], f_v[:, c, :], inp_v[:, c, :],
                carry_h[:, c:c + 1], ALU.mult, ALU.add),
                (m_toks[8 + c], tch) + (h_war[s % 2] or ()), flush=True)
            ta_ = p.op("vector", lambda e, c=c: e.tensor_tensor_scan(
                a_buf[:, c * SEG:(c + 1) * SEG], f_v[:, c, :], zseg,
                carry_a[:, c:c + 1], ALU.mult, ALU.add),
                (m_toks[8 + c], tzs, tca) + ((a_war,) if a_war else ()), flush=True)
            scan_toks += [th, ta_]
        tcu = p.op("vector", lambda e, s=s: e.tensor_copy(
            carry_h[:, :],
            h_v[s % 2][:, :, SEG - 1:SEG].rearrange("p c one -> p (c one)")),
            tuple(scan_toks))
        tcau = p.op("vector", lambda e: e.tensor_copy(
            carry_a[:, :],
            a_buf[:, :].rearrange("p (c t) -> p c t", t=SEG)[:, :, SEG - 1:SEG]
            .rearrange("p c one -> p (c one)")), tuple(scan_toks))
        td1_ = p.dma("gpsimd", lambda e, s=s: e.dma_start(
            out=hs_o.rearrange("p (c t) -> p c t", c=8)[:, :, s * SEG:(s + 1) * SEG],
            in_=h_v[s % 2][:, :, :]), dOUT, tuple(scan_toks))
        td2_ = p.dma("gpsimd", lambda e, s=s: e.dma_start(
            out=as_o.rearrange("p (c t) -> p c t", c=8)[:, :, s * SEG:(s + 1) * SEG],
            in_=a_buf[:, :].rearrange("p (c t) -> p c t", t=SEG)), dOUT2, tuple(scan_toks))
        td3_ = p.dma("gpsimd", lambda e, s=s: e.dma_start(
            out=og_o.rearrange("p (c t) -> p c t", c=8)[:, :, s * SEG:(s + 1) * SEG],
            in_=og_v[s % 2][:, :, :]), dOUT, tuple(m_toks[16:24]))
        h_war[s % 2] = (td1_, tcu)
        og_war[s % 2] = td3_
        a_war = td2_
        prev_scan_last = scan_toks[-1]
        g_final = [td1_, td2_, td3_, tcu, tcau] + scan_toks
    t_hff = p.fence("vector", (g_final[3],))
    t_hf = p.dma("gpsimd", lambda e: e.dma_start(out=hf_o, in_=carry_h[:, :]), dOUT,
                 (t_hff,))
    g_gate = tuple(g_final[0:5]) + (t_hf,)
    tsnap4 = p.op("vector", lambda e: e.tensor_copy(snap[:, 8:11], sc11[:, 0:3]), g_gate)

    if stop == "G":
        return _fin(p, g_gate + (tg6, ta8), sc11, wstat, x_tm, dbgs_o, dbgx_o, dOUT)
    hsh_v = AR1[:, 0:8192].bitcast(BF16).rearrange("p (m t) -> p m t", m=16)
    sg_bufs = [AR3[:, j * 1024:(j + 1) * 1024] for j in range(2)]
    su_bufs = [AR3[:, 2048 + j * 1024:2048 + (j + 1) * 1024] for j in range(2)]
    sd_buf = AR3[:, 4096:12288]
    i_toks = []
    sgl_war = {}
    sd_consumer = None
    for half in range(2):
        hm_flat = []
        for m in range(16):
            sgb, sub = sg_bufs[m % 2], su_bufs[m % 2]
            gate0 = (g_gate + tuple(p2_last) + all_e + all_cfm) if (half == 0 and m < 2) else ()
            war_g = (sgl_war.get(m % 2),)
            tlg = p.dma("gpsimd", lambda e, m=m, sgb=sgb: e.dma_start(
                out=sgb.rearrange("p (k c) -> p k c", k=8),
                in_=sg_d[m].rearrange("(k p) c -> p k c", p=128)), dW[1], gate0 + war_g)
            tlu = p.dma("gpsimd", lambda e, m=m, sub=sub: e.dma_start(
                out=sub.rearrange("p (k c) -> p k c", k=8),
                in_=su_d[m].rearrange("(k p) c -> p k c", p=128)), dW[2], gate0 + war_g)
            mm_pair = []
            for tt_ in range(2):
                t4 = half * 2 + tt_
                tmg = p.mm(ps[0][:, :],
                           lambda k, sgb=sgb: sgb.rearrange("p (k c) -> p k c", k=8)[:, k, :],
                           lambda k, t4=t4: ar2(k, t4 * 512, (t4 + 1) * 512), 8,
                           deps=(tlg,) + t_xn1)
                tmu = p.mm(ps[1][:, :],
                           lambda k, sub=sub: sub.rearrange("p (k c) -> p k c", k=8)[:, k, :],
                           lambda k, t4=t4: ar2(k, t4 * 512, (t4 + 1) * 512), 8,
                           deps=(tlu,) + t_xn1)
                tsl = p.op("scalar", lambda e, m=m, tt_=tt_: e.activation(
                    hsh_v[:, m, tt_ * 512:(tt_ + 1) * 512], ps[0][:, :], AF.Silu),
                    (tmg,) + (gate0 if tt_ == 0 else ()))
                p.evac(ps[0].name, tsl)
                tue = p.op("vector", lambda e, tt_=tt_: e.tensor_copy(
                    us_scr[tt_], ps[1][:, :]), (tmu,))
                p.evac(ps[1].name, tue)
                tmul = p.op("gpsimd", lambda e, m=m, tt_=tt_: e.tensor_mul(
                    hsh_v[:, m, tt_ * 512:(tt_ + 1) * 512],
                    hsh_v[:, m, tt_ * 512:(tt_ + 1) * 512], us_scr[tt_]),
                    (tsl, tue))
                mm_pair.append(tmul)
            sgl_war[m % 2] = mm_pair[-1]
            hm_flat += mm_pair
        for n in range(2):
            tld = p.dma("gpsimd", lambda e, n=n: e.dma_start(
                out=sd_buf.rearrange("p (k c) -> p k c", k=16),
                in_=sd_d[n].rearrange("(k p) c -> p k c", p=128)), dW[3],
                ((sd_consumer,) if sd_consumer else (g_gate if half == 0 else ())))
            for mt in range(8):
                ti = half * 8 + mt
                psd = ps[2 + (mt % 2)]
                tmd = p.mm(psd[:, :],
                           lambda k, mt=mt: hsh_v[:, k, mt * 128:(mt + 1) * 128],
                           lambda k: sd_buf.rearrange("p (k c) -> p k c", k=16)[:, k, :],
                           16, deps=(tld,) + tuple(hm_flat))
                tev = p.op("vector", lambda e, ti=ti, n=n, psd=psd: e.scalar_tensor_tensor(
                    xs_(ti, n * 512, (n + 1) * 512), psd[:, :], 0.0,
                    xs_(ti, n * 512, (n + 1) * 512), ALU.add, ALU.add),
                    (tmd, xgate))
                p.evac(psd.name, tev)
                i_toks.append(tev)
                sd_consumer = tmd
    all_i = tuple(i_toks)

    if stop == "I":
        return _fin(p, all_i + (tg6, ta8), sc11, wstat, x_tm, dbgs_o, dbgx_o, dOUT)
    t_ohr = p.dma("gpsimd", lambda e: e.dma_start(out=oh_r, in_=oh_d), dIN, g_gate)
    he_v = AR1[:, 0:8192].bitcast(BF16).rearrange("p (q t) -> p q t", q=32)
    guw_bufs = [AR3[:, j * 4096:(j + 1) * 4096] for j in range(2)]
    dw_bufs = [AR3[:, 8192 + j * 2048:8192 + (j + 1) * 2048] for j in range(2)]
    j_evac_toks = []
    gw_war = {}
    uw_war = {}
    dw_war = {}
    cb_war = {}
    he_war = None
    for t4 in range(4):
        p1_he = {}
        for ex in range(16):
            gwb = guw_bufs[ex % 2]
            gate0 = (all_i + (sd_consumer,)) if (t4 == 0 and ex < 2) else ()
            tlg = p.dma("gpsimd", lambda e, ex=ex, gwb=gwb: e.dma_start(
                out=gwb.rearrange("p (gu hc k c) -> p gu hc k c", gu=2, hc=2, k=8),
                in_=guw_d[ex].rearrange("gu hc (k p) c -> p gu hc k c", p=128)),
                dW[1 + (ex % 2)], gate0 + (gw_war.get(ex % 2),))
            tlu = tlg
            tcbm = p.mm(ps[4][:, :], lambda k, ex=ex: oh_r[:, ex * 128:(ex + 1) * 128],
                        lambda k, t4=t4: C_fm[:, t4 * 512:(t4 + 1) * 512], 1,
                        deps=all_cfm + (t_ohr,))
            tcbe = p.op("scalar", lambda e, ex=ex: e.copy(cbw_scr[ex % 2], ps[4][:, :]),
                        (tcbm, cb_war.get(ex % 2), f_stat_toks[-1]))
            p.evac(ps[4].name, tcbe)
            glast = None
            for hc in range(2):
                q_ = ex * 2 + hc
                tmg = p.mm(ps[0][:, :],
                           lambda k, gwb=gwb, hc=hc: gwb.rearrange(
                               "p (gu hc k c) -> p gu hc k c", gu=2, hc=2, k=8)[:, 0, hc, k, :],
                           lambda k, t4=t4: ar2(k, t4 * 512, (t4 + 1) * 512), 8,
                           deps=(tlg,) + t_xn1)
                tmu = p.mm(ps[1][:, :],
                           lambda k, gwb=gwb, hc=hc: gwb.rearrange(
                               "p (gu hc k c) -> p gu hc k c", gu=2, hc=2, k=8)[:, 1, hc, k, :],
                           lambda k, t4=t4: ar2(k, t4 * 512, (t4 + 1) * 512), 8,
                           deps=(tlu,) + t_xn1)
                tsl = p.op("scalar", lambda e, q_=q_: e.activation(
                    he_v[:, q_, :], ps[0][:, :], AF.Silu),
                    (tmg,) + ((he_war,) if he_war and t4 > 0 else ())
                    + (gate0 if (hc == 0 and ex < 2) else ()))
                p.evac(ps[0].name, tsl)
                tue = p.op("vector", lambda e, hc=hc: e.tensor_copy(
                    us_scr[hc], ps[1][:, :]), (tmu,))
                p.evac(ps[1].name, tue)
                tm1 = p.op("gpsimd", lambda e, q_=q_, hc=hc: e.tensor_mul(
                    he_v[:, q_, :], he_v[:, q_, :], us_scr[hc]), (tsl, tue))
                tm2 = p.op("gpsimd", lambda e, q_=q_, ex=ex: e.tensor_mul(
                    he_v[:, q_, :], he_v[:, q_, :], cbw_scr[ex % 2]), (tm1, tcbe), flush=True)
                p1_he[q_] = tm2
                glast = tm2
            gw_war[ex % 2] = glast
            uw_war[ex % 2] = glast
            cb_war[ex % 2] = glast
        grp_toks = {}
        for ex in range(16):
            dwb = dw_bufs[ex % 2]
            tld = p.dma("gpsimd", lambda e, ex=ex, dwb=dwb: e.dma_start(
                out=dwb.rearrange("p (n k c) -> p n k c", n=2, k=2),
                in_=dw_d[ex].rearrange("n (k p) c -> p n k c", p=128)), dW[3],
                (dw_war.get(ex % 2),) + ((all_i[-1],) if t4 == 0 and ex < 2 else ()))
            glast = None
            for mt in range(4):
                for n2 in range(2):
                    psj = ps[mt * 2 + n2]

                    def dfn(e, ex=ex, mt=mt, n2=n2, dwb=dwb, t4=t4):
                        last = None
                        for hc in range(2):
                            last = e.matmul(
                                ps[mt * 2 + n2][:, :],
                                he_v[:, ex * 2 + hc, mt * 128:(mt + 1) * 128],
                                dwb.rearrange("p (n k c) -> p n k c", n=2, k=2)[:, n2, hc, :],
                                start=(ex == 0 and hc == 0), stop=(ex == 15 and hc == 1))
                        return last
                    deps = (tld, p1_he[ex * 2], p1_he[ex * 2 + 1])
                    if ex == 0 and psj.name in p.bank_tok:
                        deps += (p.bank_tok[psj.name],)
                    tokd = p.op("tensor", dfn, deps)
                    grp_toks[(mt, n2)] = tokd
                    glast = tokd
            dw_war[ex % 2] = glast
        he_war = None
        ev_this = []
        for mt in range(4):
            for n2 in range(2):
                ti = t4 * 4 + mt
                psj = ps[mt * 2 + n2]
                tev = p.op("vector", lambda e, ti=ti, n2=n2, psj=psj: e.scalar_tensor_tensor(
                    xs_(ti, n2 * 512, (n2 + 1) * 512), psj[:, :], 0.0,
                    xs_(ti, n2 * 512, (n2 + 1) * 512), ALU.add, ALU.add),
                    (grp_toks[(mt, n2)], xgate))
                p.evac(psj.name, tev)
                j_evac_toks.append(tev)
                ev_this.append(tev)
        he_war = ev_this[-1]

    tsnap5 = p.op("vector", lambda e: e.tensor_copy(snap[:, 11:14], sc11[:, 0:3]), tuple(j_evac_toks))
    p.dma("gpsimd", lambda e: e.dma_start(out=dbgs_o[0:1, :], in_=wstat[:, :]), dBN, (ta8,))
    p.dma("gpsimd", lambda e: e.dma_start(out=dbgs_o[1:2, :], in_=snap[:, 0:8]), dBN, (tsnap1, tsnap2, tsnap3, tsnap4, tsnap5))
    p.dma("gpsimd", lambda e: e.dma_start(out=dbgx_o[0:1, 16:32], in_=snap[:, :]), dBN, (tsnap1, tsnap2, tsnap3, tsnap4, tsnap5))
    if dbg == "postD":
        p.dma("gpsimd", lambda e: e.dma_start(out=dbgx_o, in_=x_tm[:, :]), dBN, all_d)
    elif dbg == "postE":
        p.dma("gpsimd", lambda e: e.dma_start(out=dbgx_o, in_=x_tm[:, :]), dBN, all_e)
    elif dbg == "postI":
        p.dma("gpsimd", lambda e: e.dma_start(out=dbgx_o, in_=x_tm[:, :]), dBN, all_i)
    else:
        p.dma("gpsimd", lambda e: e.dma_start(out=dbgx_o[:, 0:16], in_=x_tm[:, 0:16]), dBN, all_d)
    p.dma("gpsimd", lambda e: e.dma_start(out=xp_o, in_=x_tm[:, :]), dOUT,
          tuple(j_evac_toks) + all_i + all_d + all_e)
    return p.build()


def run_launch2(ws, xs, l1res, kv_pairs, gmaxv, cores=None, dbg=None, stop=None):
    nc = build_launch2(dbg, stop)
    shared = dict(woT_slab=ws["woT_slab"], b_out_row=ws["b_out_row"],
                  wffnT=ws["wffnT"], w1_slab=ws["w1_slab"], b1_fm=ws["b1_fm"],
                  rwT=ws["rwT"], cb_b=ws["cb_b"], guw_slab=ws["guw_slab"],
                  dw_slab=ws["dw_slab"],
                  sgT_slab=ws["sgT_slab"], suT_slab=ws["suT_slab"],
                  sdT_slab=ws["sdT_slab"], onehot=ws["onehot"],
                  selhead=ws["selhead"], ident=ws["ident"])
    in_maps = []
    for c in range(NCORE):
        in_maps.append(dict(
            shared, x=xs[c], q_fm=np.asarray(l1res[c]["q_fm"]),
            xn2=np.asarray(l1res[c]["xn2"]),
            kv_self=np.asarray(l1res[c]["kvp"]),
            kv_peer=np.asarray(l1res[c ^ 1]["kvp"]),
            gmaxv=gmaxv))
    return _run(nc, in_maps)



def build_launch3():
    p = Prog()
    nc = p.nc
    xp_d = nc.dram_tensor("xpf", [128, 8 * T], F32, kind="ExternalInput").ap()
    h_d = nc.dram_tensor("hso", [128, 8 * T], F32, kind="ExternalInput").ap()
    a_d = nc.dram_tensor("aso", [128, 8 * T], F32, kind="ExternalInput").ap()
    og_d = nc.dram_tensor("ogo", [128, 8 * T], F32, kind="ExternalInput").ap()
    cr_d = nc.dram_tensor("carry", [128, 8], F32, kind="ExternalInput").ap()
    x2_o = nc.dram_tensor("x2f", [128, 8 * T], F32, kind="ExternalOutput").ap()

    xp_c = [p.sb(f"xp{j}", [128, T], F32) for j in range(2)]
    h_c = [p.sb(f"h{j}", [128, T], F32) for j in range(2)]
    a_c = [p.sb(f"a{j}", [128, T], F32) for j in range(2)]
    og_c = [p.sb(f"og{j}", [128, T], F32) for j in range(2)]
    t_c = [p.sb(f"t{j}", [128, T], F32) for j in range(2)]
    cr = p.sb("cr", [128, 8], F32)
    dI = [p.dma_sem(f"dI{j}") for j in range(4)]
    dO = p.dma_sem("dO")

    t_cr = p.dma("sync", lambda e: e.dma_start(out=cr[:, :], in_=cr_d), dI[0])
    prev = {}
    for c in range(8):
        j = c % 2
        war = (prev.get(j),)
        tl1 = p.dma("sync", lambda e, c=c, j=j: e.dma_start(
            out=xp_c[j][:, :], in_=xp_d[:, c * T:(c + 1) * T]), dI[0], war)
        tl2 = p.dma("sync", lambda e, c=c, j=j: e.dma_start(
            out=h_c[j][:, :], in_=h_d[:, c * T:(c + 1) * T]), dI[1], war)
        tl3 = p.dma("gpsimd", lambda e, c=c, j=j: e.dma_start(
            out=a_c[j][:, :], in_=a_d[:, c * T:(c + 1) * T]), dI[2], war)
        tl4 = p.dma("gpsimd", lambda e, c=c, j=j: e.dma_start(
            out=og_c[j][:, :], in_=og_d[:, c * T:(c + 1) * T]), dI[3], war)
        t1 = p.op("vector", lambda e, c=c, j=j: e.scalar_tensor_tensor(
            h_c[j][:, :], a_c[j][:, :], cr[:, c:c + 1], h_c[j][:, :],
            ALU.mult, ALU.add), (tl2, tl3, t_cr))
        t2 = p.op("scalar", lambda e, j=j: e.activation(
            t_c[j][:, :], h_c[j][:, :], AF.Tanh), (t1, prev.get(j)))
        t3 = p.op("vector", lambda e, j=j: e.tensor_mul(
            t_c[j][:, :], t_c[j][:, :], og_c[j][:, :]), (t2, tl4))
        t4 = p.op("vector", lambda e, j=j: e.tensor_add(
            xp_c[j][:, :], xp_c[j][:, :], t_c[j][:, :]), (t3, tl1))
        to = p.dma("gpsimd", lambda e, c=c, j=j: e.dma_start(
            out=x2_o[:, c * T:(c + 1) * T], in_=xp_c[j][:, :]), dO, (t4,))
        prev[j] = to
    return p.build()


_NC_CACHE = {}


def _get_nc(name, builder):
    if name not in _NC_CACHE:
        _NC_CACHE[name] = builder()
    return _NC_CACHE[name]


def _run(nc, in_maps):
    res = run_bass_kernel_spmd(nc, in_maps, core_ids=list(range(NCORE)))
    return res.results


def kernel(**inputs):
    ws, xs = prep_host(inputs)

    nc1 = _get_nc("l1", build_launch1)
    sh1 = dict(wq_slab=ws["wq_slab"], wkv_slab=ws["wkv_slab"],
               gamma_b=ws["gamma_b"], beta_b=ws["beta_b"])
    r1 = _run(nc1, [dict(sh1, x=xs[c]) for c in range(NCORE)])

    gmaxv = np.ascontiguousarray(
        np.array([[np.asarray(r1[c]["maxloc"])[0, 0] for c in range(NCORE)]],
                 np.float32))

    nc2 = _get_nc("l2", build_launch2)
    sh2 = dict(woT_slab=ws["woT_slab"], b_out_row=ws["b_out_row"],
               wffnT=ws["wffnT"], w1_slab=ws["w1_slab"], b1_fm=ws["b1_fm"],
               rwT=ws["rwT"], cb_b=ws["cb_b"], guw_slab=ws["guw_slab"],
               dw_slab=ws["dw_slab"],
               sgT_slab=ws["sgT_slab"], suT_slab=ws["suT_slab"],
               sdT_slab=ws["sdT_slab"], onehot=ws["onehot"],
               selhead=ws["selhead"], ident=ws["ident"])
    in2 = [dict(sh2, x=xs[c], q_fm=np.asarray(r1[c]["q_fm"]),
                xn2=np.asarray(r1[c]["xn2"]),
                kv_self=np.asarray(r1[c]["kvp"]),
                kv_peer=np.asarray(r1[c ^ 1]["kvp"]),
                gmaxv=gmaxv) for c in range(NCORE)]
    r2 = _run(nc2, in2)

    in3 = []
    for c in range(NCORE):
        xp = np.asarray(r2[c]["xp"])
        xpt = np.ascontiguousarray(
            xp.reshape(128, NT, 1024).transpose(2, 0, 1))
        xp_fm = np.ascontiguousarray(
            xpt.reshape(8, 128, 128, NT).transpose(1, 0, 3, 2).reshape(128, 8 * T))
        if c % 2 == 1:
            carry = np.ascontiguousarray(np.asarray(r2[c - 1]["hfin"], np.float32))
        else:
            carry = np.zeros((128, 8), np.float32)
        in3.append(dict(xpf=xp_fm, hso=np.asarray(r2[c]["hso"]),
                        aso=np.asarray(r2[c]["aso"]), ogo=np.asarray(r2[c]["ogo"]),
                        carry=carry))

    nc3 = _get_nc("l3", build_launch3)
    r3 = _run(nc3, in3)

    out = np.zeros((NCORE * T, D), np.float32)
    for c in range(NCORE):
        x2 = np.asarray(r3[c]["x2f"])
        blk = x2.reshape(128, 8, T).transpose(2, 1, 0).reshape(T, D)
        out[c * T:(c + 1) * T] = blk
    return out.reshape(4, 4096, 1024)


def run_launch1(ws, xs):
    nc = build_launch1()
    shared = dict(wq_slab=ws["wq_slab"], wkv_slab=ws["wkv_slab"],
                  gamma_b=ws["gamma_b"], beta_b=ws["beta_b"])
    in_maps = [dict(shared, x=xs[c]) for c in range(NCORE)]
    return _run(nc, in_maps)





# version 07
# speedup vs baseline: 1.3377x; 1.3377x over previous
import numpy as np
import ml_dtypes
from contextlib import ExitStack

import concourse.bass as bass
import concourse.mybir as mybir
from concourse.bass_utils import run_bass_kernel_spmd

F32 = mybir.dt.float32
F32R = mybir.dt.float32r
BF16 = mybir.dt.bfloat16
I32 = mybir.dt.int32
AF = mybir.ActivationFunctionType
ALU = mybir.AluOpType
AX = mybir.AxisListType
BF = ml_dtypes.bfloat16

NCORE = 8
T = 2048
D = 1024
NT = 16
NCK = 8
H, HD = 16, 64
EPS = 1e-5

ENGS = ("vector", "scalar", "tensor", "gpsimd", "sync")


class DmaSem:
    def __init__(self, name):
        self.name = name
        self.n = 0
        self.h = None


class Prog:

    def __init__(self):
        self.nc = bass.Bass("TRN2", target_bir_lowering=False)
        self.steps = []
        self.ecount = {e: 0 for e in ENGS}
        self.dsems = {}
        self.stack = ExitStack()
        self.bank_tok = {}
        self._junk = {}

    def dma_sem(self, name):
        s = DmaSem(name)
        self.dsems[name] = s
        return s

    def sb(self, name, shape, dtype, **kw):
        return self.stack.enter_context(self.nc.sbuf_tensor(name, shape, dtype, **kw))

    def ps(self, name, shape=(128, 512), dtype=F32):
        return self.stack.enter_context(self.nc.psum_tensor(name, list(shape), dtype))

    def op(self, eng, fn, deps=(), flush=False):
        self.ecount[eng] += 1
        tok = ("e", eng, self.ecount[eng])
        self.steps.append((eng, fn, tuple(d for d in deps if d is not None)
                           + (("f", eng, self.ecount[eng] - 1),) * flush, None, tok))
        return tok

    def fence(self, eng, deps=()):
        jt = self._junk.get(eng)
        if jt is None:
            jt = self.sb(f"junk_{eng}", [1, 8], F32)
            self._junk[eng] = jt
        return self.op(eng, lambda e, jt=jt: e.memzero(jt[:, :]), deps)

    def dma(self, eng, fn, sem, deps=()):
        deps = tuple(d for d in deps if d is not None)
        if sem.n:
            deps += (("d", sem.name, sem.n),)
        sem.n += 1
        tok = ("d", sem.name, sem.n)
        self.steps.append((eng, fn, deps, sem, tok))
        return tok

    def mm(self, psum_ap, lhsT_fn, rhs_fn, nk, deps=(), start=True, stop=True,
           extras=()):
        name = psum_ap.tensor.name

        def fn(e):
            last = None
            for k in range(nk):
                last = e.matmul(psum_ap, lhsT_fn(k), rhs_fn(k),
                                start=(start and k == 0),
                                stop=(stop and not extras and k == nk - 1))
            for j, ex in enumerate(extras):
                last = ex(e)
            return last
        deps = tuple(deps)
        if name in self.bank_tok:
            deps += (self.bank_tok[name],)
        return self.op("tensor", fn, deps)

    def evac(self, psum_name, tok):
        self.bank_tok[psum_name] = tok

    def build(self):
        nc = self.nc
        esems = {}
        for e in ENGS:
            esems[e] = self.stack.enter_context(nc.semaphore(f"tok_{e}"))
        for s in self.dsems.values():
            s.h = self.stack.enter_context(nc.semaphore(s.name))
        block = self.stack.enter_context(nc.Block())
        steps = self.steps
        dsems = self.dsems
        ecount = self.ecount

        def run_stream(myname, eng):
            hw = {}
            for (ename, fn, deps, dsem, tok) in steps:
                if ename != myname:
                    continue
                for dep in deps:
                    kind, key, cnt = dep
                    if cnt == 0:
                        continue
                    if kind == "f":
                        sem, val = esems[myname], cnt
                    elif kind == "e":
                        if key == myname:
                            continue
                        sem, val = esems[key], cnt
                    else:
                        sem, val = dsems[key].h, 16 * cnt
                    hwk = {"e": key, "d": "d" + key, "f": myname}[kind]
                    if hw.get(hwk, -1) < val:
                        eng.wait_ge(sem, val)
                        hw[hwk] = val
                inst = fn(eng)
                if dsem is not None:
                    inst.then_inc(dsem.h, 16)
                else:
                    inst.then_inc(esems[ename], 1)
            if myname == "gpsimd":
                for e in ENGS:
                    if ecount[e] and e != "gpsimd":
                        eng.wait_ge(esems[e], ecount[e])
                for s in dsems.values():
                    if s.n:
                        eng.wait_ge(s.h, 16 * s.n)

        for name, setter in (("gpsimd", block.gpsimd), ("vector", block.vector),
                             ("scalar", block.scalar), ("tensor", block.tensor),
                             ("sync", block.sync)):
            if name == "gpsimd" or any(s[0] == name for s in steps):
                setter(lambda eng, name=name: run_stream(name, eng))
        self.stack.close()
        return nc



def _mslabs(wT, width):
    i, o = wT.shape
    return np.ascontiguousarray(wT.reshape(i, o // width, width).transpose(1, 0, 2))


def prep_host(inputs):
    ws = {}
    wqkvT = np.ascontiguousarray(np.asarray(inputs["L0_attn_w_qkv"], np.float32).T)
    ws["wq_slab"] = _mslabs(np.ascontiguousarray(wqkvT[:, 0:1024]), 128)
    ws["wkv_slab"] = _mslabs(np.ascontiguousarray(wqkvT[:, 1024:3072]), 512)
    ws["woT_slab"] = _mslabs(np.ascontiguousarray(np.asarray(inputs["L0_attn_w_out"], np.float32).T), 512)
    ws["b_out_row"] = np.ascontiguousarray(np.asarray(inputs["L0_attn_b_out"], np.float32).reshape(1, 1024))
    ws["wffnT"] = np.ascontiguousarray(np.asarray(inputs["L0_ffn_weight"], np.float32).T)
    ws["gamma_b"] = np.ascontiguousarray(np.broadcast_to(np.asarray(inputs["L0_ffn_gamma"], np.float32), (128, 1024)))
    ws["beta_b"] = np.ascontiguousarray(np.broadcast_to(np.asarray(inputs["L0_ffn_beta"], np.float32), (128, 1024)))
    w1s = _mslabs(np.ascontiguousarray(np.asarray(inputs["L1_attn_W"], np.float32)), 128)
    ws["w1_slab"] = np.ascontiguousarray(
        w1s.reshape(6, 4, 1024, 128).transpose(0, 2, 1, 3).reshape(6, 1024, 512))
    ws["b1_fm"] = np.ascontiguousarray(np.asarray(inputs["L1_attn_b"], np.float32).reshape(24, 128).T)
    ws["rwT"] = np.ascontiguousarray(np.asarray(inputs["L1_ffn_router_w"], np.float32).T)
    ws["cb_b"] = np.ascontiguousarray(np.broadcast_to(np.asarray(inputs["L1_ffn_correction_bias"], np.float32), (128, 16)))
    gw = np.asarray(inputs["L1_ffn_gate_w"], np.float32)
    uw = np.asarray(inputs["L1_ffn_up_w"], np.float32)
    dw = np.asarray(inputs["L1_ffn_down_w"], np.float32)
    gw_s = gw.reshape(16, 1024, 2, 128).transpose(0, 2, 1, 3)
    uw_s = uw.reshape(16, 1024, 2, 128).transpose(0, 2, 1, 3)
    ws["guw_slab"] = np.ascontiguousarray(np.stack([gw_s, uw_s], axis=1))
    ws["dw_slab"] = np.ascontiguousarray(dw.reshape(16, 256, 2, 512).transpose(0, 2, 1, 3))
    ws["sgT_slab"] = _mslabs(np.ascontiguousarray(np.asarray(inputs["L1_ffn_shared_gate_w"], np.float32).T), 128)
    ws["suT_slab"] = _mslabs(np.ascontiguousarray(np.asarray(inputs["L1_ffn_shared_up_w"], np.float32).T), 128)
    ws["sdT_slab"] = _mslabs(np.ascontiguousarray(np.asarray(inputs["L1_ffn_shared_down_w"], np.float32).T), 512)
    sel = np.zeros((16, 16 * 128), np.float32)
    for e in range(16):
        sel[e, e * 128:(e + 1) * 128] = 1.0
    ws["onehot"] = sel
    selh = np.zeros((8, 16, 128), np.float32)
    for c in range(8):
        selh[c, 2 * c, 0:64] = 1.0
        selh[c, 2 * c + 1, 64:128] = 1.0
    ws["selhead"] = selh
    ws["ident"] = np.eye(128, dtype=np.float32)
    x = np.ascontiguousarray(np.asarray(inputs["L0_attn_x"], np.float32).reshape(NCORE * T, D))
    xs = [np.ascontiguousarray(x[c * T:(c + 1) * T]) for c in range(NCORE)]
    return ws, xs



def build_launch1():
    p = Prog()
    nc = p.nc
    x_d = nc.dram_tensor("x", [T, D], F32, kind="ExternalInput").ap()
    wq_d = nc.dram_tensor("wq_slab", [8, 1024, 128], F32, kind="ExternalInput").ap()
    wkv_d = nc.dram_tensor("wkv_slab", [4, 1024, 512], F32, kind="ExternalInput").ap()
    gam_d = nc.dram_tensor("gamma_b", [128, 1024], F32, kind="ExternalInput").ap()
    bet_d = nc.dram_tensor("beta_b", [128, 1024], F32, kind="ExternalInput").ap()
    q_o = nc.dram_tensor("q_fm", [128, 8 * T], BF16, kind="ExternalOutput").ap()
    xn2_o = nc.dram_tensor("xn2", [128, NT * 1024], F32, kind="ExternalOutput").ap()
    kv_o = nc.dram_tensor("kvp", [128, 520], F32, kind="ExternalOutput").ap()
    mx_o = nc.dram_tensor("maxloc", [1, 1], F32, kind="ExternalOutput").ap()
    bounce = nc.dram_tensor("bounce", [1, 128], F32).ap()

    x_tm = p.sb("x_tm", [128, NT * 1024], F32)
    xn_fm = p.sb("xn_fm", [128, 8 * T], BF16, is_transpose_dest=True)
    wq_sb = p.sb("wq_sb", [128, 8, 1024], BF16)
    wkv_sb = p.sb("wkv_sb", [128, 8, 2048], BF16)
    gam_sb = p.sb("gam_sb", [128, 1024], F32)
    bet_sb = p.sb("bet_sb", [128, 1024], F32)
    s1 = p.sb("s1", [128, NT], F32)
    s2 = p.sb("s2", [128, NT], F32)
    mean = p.sb("mean", [128, NT], F32)
    rstd = p.sb("rstd", [128, NT], F32)
    vscr = p.sb("vscr", [128, NT], F32)
    mean2 = p.sb("mean2", [128, NT], F32)
    rstd2 = p.sb("rstd2", [128, NT], F32)
    mxc = p.sb("mxc", [128, NT], F32)
    dump = p.sb("dump", [128, 1024], F32)
    cbw_scr = [dump[:, j * 512:(j + 1) * 512] for j in range(2)]
    xn_scr = [p.sb(f"xn_scr{j}", [128, 1024], F32) for j in range(2)]
    xnb_scr = [p.sb(f"xnb_scr{j}", [128, 1024], BF16) for j in range(2)]
    u_scr = [p.sb(f"u_scr{j}", [128, 1024], F32) for j in range(2)]
    ev_scr = [p.sb(f"ev_scr{j}", [128, 512], F32) for j in range(4)]
    q_strip = [p.sb(f"q_strip{j}", [128, 2048], BF16) for j in range(2)]
    k_t = p.sb("k_t", [128, 1024], BF16)
    v_t = p.sb("v_t", [128, 1024], BF16)
    ones_bf = p.sb("ones_bf", [128, 1], BF16)
    kvs = p.sb("kvs", [128, 520], F32)
    red = p.sb("red", [128, 1], F32)
    redT = p.sb("redT", [1, 128], F32)
    gmx = p.sb("gmx", [1, 1], F32)

    ps_q = [p.ps(f"ps_q{j}") for j in range(2)]
    ps_kA = [p.ps(f"ps_kA{j}") for j in range(2)]
    ps_kB = [p.ps(f"ps_kB{j}") for j in range(2)]
    ps_kv = p.ps("ps_kv")
    ps_ks = p.ps("ps_ks")

    dIN = p.dma_sem("dIN")
    dIN2 = p.dma_sem("dIN2")
    dTR = p.dma_sem("dTR")
    dOUT = p.dma_sem("dOUT")
    dBN = p.dma_sem("dBN")

    def xt(i):
        return x_tm[:, i * 1024:(i + 1) * 1024]

    def xnf(c, a, b):
        return xn_fm[:, c * T + a:c * T + b]

    t_x = p.dma("sync", lambda e: e.dma_start(
        out=x_tm[:, :].rearrange("p (i f) -> p i f", i=NT),
        in_=x_d.rearrange("(i p) f -> p i f", p=128)), dIN2)
    t_gam = p.dma("sync", lambda e: e.dma_start(out=gam_sb[:, :], in_=gam_d), dIN2)
    t_bet = p.dma("sync", lambda e: e.dma_start(out=bet_sb[:, :], in_=bet_d), dIN2)
    t_wq = [p.dma("gpsimd", lambda e, m=m: e.dma_start(
        out=wq_sb[:, :, m * 128:(m + 1) * 128],
        in_=wq_d[m].rearrange("(k p) c -> p k c", p=128)), dIN) for m in range(8)]
    t_wkv = [p.dma("gpsimd", lambda e, n=n: e.dma_start(
        out=wkv_sb[:, :, n * 512:(n + 1) * 512],
        in_=wkv_d[n].rearrange("(k p) c -> p k c", p=128)), dIN) for n in range(4)]
    t_ones = p.op("vector", lambda e: e.memset(ones_bf[:, :], 1.0))

    def ln_block(src_fn, src_deps, mean_t, rstd_t):
        toks = []
        for i in range(NT):
            dep_i = src_deps(i)
            toks.append(p.op("vector", lambda e, i=i: e.reduce_sum(
                s1[:, i:i + 1], src_fn(i), axis=AX.X), dep_i))
            toks.append(p.op("scalar", lambda e, i=i: e.activation(
                dump[:, :], src_fn(i), AF.Square, accum_out=s2[:, i:i + 1]), dep_i))
        t1 = p.op("vector", lambda e: e.tensor_scalar_mul(mean_t[:, :], s1[:, :], 1.0 / D), toks)
        t2 = p.op("vector", lambda e: e.tensor_scalar_mul(vscr[:, :], s2[:, :], 1.0 / D), toks)
        t3 = p.op("vector", lambda e: e.tensor_mul(s1[:, :], mean_t[:, :], mean_t[:, :]), (t1,), flush=True)
        t4 = p.op("vector", lambda e: e.tensor_sub(vscr[:, :], vscr[:, :], s1[:, :]), (t2, t3), flush=True)
        t5 = p.op("vector", lambda e: e.tensor_scalar_add(vscr[:, :], vscr[:, :], EPS), (t4,), flush=True)
        t6 = p.op("scalar", lambda e: e.sqrt(s2[:, :], vscr[:, :]), (t5,))
        t7 = p.op("vector", lambda e: e.reciprocal(rstd_t[:, :], s2[:, :]), (t6,), flush=True)
        return (t7,), toks

    t_ln1, ln1_tile_toks = ln_block(xt, lambda i: (t_x,), mean, rstd)

    tr_toks = []
    for i in range(NT):
        sc, scb = xn_scr[i % 2], xnb_scr[i % 2]
        war = (tr_toks[-2],) if len(tr_toks) >= 2 else ()
        tn = p.op("vector", lambda e, i=i, sc=sc: e.tensor_scalar(
            sc[:, :], xt(i), mean[:, i:i + 1], rstd[:, i:i + 1],
            ALU.subtract, ALU.mult), t_ln1 + (t_x,) + war)
        tc = p.op("gpsimd", lambda e, sc=sc, scb=scb: e.tensor_copy(scb[:, :], sc[:, :]), (tn,))
        tt = p.dma("sync", lambda e, i=i, scb=scb: e.dma_start_transpose(
            xn_fm[:, :].rearrange("p (c t) -> p c t", c=8)[:, :, i * 128:(i + 1) * 128],
            scb[:, :]), dTR, (tc,))
        tr_toks.append(tt)
    t_xn = tr_toks[-1]

    elu_war = {}
    elu_war0 = {}

    def elu1_evac(ps_ap, out_ap, j, deps):
        e0, e1 = ev_scr[2 * j], ev_scr[2 * j + 1]
        ta = p.op("vector", lambda e: e.tensor_scalar_min(e0[:, :], ps_ap, 0.0),
                  tuple(deps) + (elu_war0.get(j),))
        tb = p.op("scalar", lambda e: e.activation(e1[:, :], e0[:, :], AF.Exp),
                  (ta, elu_war.get(j)))
        tc = p.op("vector", lambda e: e.scalar_tensor_tensor(
            out_ap, ps_ap, 0.0, e1[:, :], ALU.max, ALU.add), (tb,))
        elu_war[j] = tc
        elu_war0[j] = tb
        return tc

    q_dma_toks = []
    for m in range(8):
        strip = q_strip[m % 2]
        strip_war = q_dma_toks[-2] if len(q_dma_toks) >= 2 else None
        ev_toks = []
        for t4 in range(4):
            psq = ps_q[t4 % 2]
            tm = p.mm(psq[:, :],
                      lambda k, m=m: wq_sb[:, k, m * 128:(m + 1) * 128],
                      lambda k, t4=t4: xnf(k, t4 * 512, (t4 + 1) * 512),
                      8, deps=(t_xn, t_wq[m]))
            te = elu1_evac(psq[:, :], strip[:, t4 * 512:(t4 + 1) * 512], t4 % 2,
                           (tm, strip_war))
            p.evac(psq.name, te)
            ev_toks.append(te)
        q_dma_toks.append(p.dma("gpsimd", lambda e, m=m, strip=strip: e.dma_start(
            out=q_o[:, m * T:(m + 1) * T], in_=strip[:, :]), dOUT, ev_toks))

    kv_grp_tok = None
    for ti in range(NT):
        k_ev = []
        v_ev = []
        for n in range(2):
            tmk = p.mm(ps_kA[n][:, :],
                       lambda k, ti=ti: xnf(k, ti * 128, (ti + 1) * 128),
                       lambda k, n=n: wkv_sb[:, k, n * 512:(n + 1) * 512],
                       8, deps=(t_xn, t_wkv[n], kv_grp_tok))
            te = elu1_evac(ps_kA[n][:, :], k_t[:, n * 512:(n + 1) * 512], n,
                           (tmk, kv_grp_tok))
            p.evac(ps_kA[n].name, te)
            k_ev.append(te)
            tmv = p.mm(ps_kB[n][:, :],
                       lambda k, ti=ti: xnf(k, ti * 128, (ti + 1) * 128),
                       lambda k, n=n: wkv_sb[:, k, 1024 + n * 512:1024 + (n + 1) * 512],
                       8, deps=(t_xn, t_wkv[2 + n], kv_grp_tok))
            tev = p.op("scalar", lambda e, n=n: e.copy(
                v_t[:, n * 512:(n + 1) * 512], ps_kB[n][:, :]), (tmv, kv_grp_tok))
            p.evac(ps_kB[n].name, tev)
            v_ev.append(tev)

        def kv_fn(e, ti=ti):
            last = None
            for h in range(H):
                j, base = h // 2, 64 * (h % 2)
                last = e.matmul(ps_kv[base:base + 64, 64 * j:64 * (j + 1)],
                                k_t[:, h * 64:(h + 1) * 64], v_t[:, h * 64:(h + 1) * 64],
                                start=(ti == 0 and h == 0), stop=(ti == NT - 1 and h == H - 1))
                last = e.matmul(ps_ks[base:base + 64, j:j + 1],
                                k_t[:, h * 64:(h + 1) * 64], ones_bf[:, :],
                                start=(ti == 0 and h == 0), stop=(ti == NT - 1 and h == H - 1))
            return last
        kv_grp_tok = p.op("tensor", kv_fn, tuple(k_ev + v_ev) + ((t_ones,) if ti == 0 else ()))

    tkv1 = p.op("vector", lambda e: e.tensor_copy(kvs[:, 0:512], ps_kv[:, :]), (kv_grp_tok,))
    tkv2 = p.op("vector", lambda e: e.tensor_copy(kvs[:, 512:520], ps_ks[:, 0:8]), (kv_grp_tok,))
    t_kv_o = p.dma("gpsimd", lambda e: e.dma_start(out=kv_o, in_=kvs[:, :]), dOUT, (tkv1, tkv2))

    def xn_src(i):
        sc = u_scr[i % 2]
        tn = p.op("vector", lambda e, i=i, sc=sc: e.tensor_scalar(
            sc[:, :], xt(i), mean[:, i:i + 1], rstd[:, i:i + 1],
            ALU.subtract, ALU.mult), t_ln1 + xn_src.war.get(i % 2, ()))
        return sc, tn

    xn_src.war = {}
    ln2_tile_toks = []
    for i in range(NT):
        sc, tn = xn_src(i)
        ta = p.op("vector", lambda e, i=i, sc=sc: e.reduce_sum(
            s1[:, i:i + 1], sc[:, :], axis=AX.X), (tn,))
        tb = p.op("scalar", lambda e, i=i, sc=sc: e.activation(
            dump[:, :], sc[:, :], AF.Square, accum_out=s2[:, i:i + 1]), (tn,))
        xn_src.war[i % 2] = (ta, tb)
        ln2_tile_toks += [ta, tb]
    t1 = p.op("vector", lambda e: e.tensor_scalar_mul(mean2[:, :], s1[:, :], 1.0 / D), ln2_tile_toks)
    t2 = p.op("vector", lambda e: e.tensor_scalar_mul(vscr[:, :], s2[:, :], 1.0 / D), ln2_tile_toks)
    t3 = p.op("vector", lambda e: e.tensor_mul(s1[:, :], mean2[:, :], mean2[:, :]), (t1,), flush=True)
    t4_ = p.op("vector", lambda e: e.tensor_sub(vscr[:, :], vscr[:, :], s1[:, :]), (t2, t3), flush=True)
    t5 = p.op("vector", lambda e: e.tensor_scalar_add(vscr[:, :], vscr[:, :], EPS), (t4_,), flush=True)
    t6 = p.op("scalar", lambda e: e.sqrt(s2[:, :], vscr[:, :]), (t5,))
    t_ln2 = p.op("vector", lambda e: e.reciprocal(rstd2[:, :], s2[:, :]), (t6,), flush=True)

    mx_toks = []
    xn2_toks = []
    for i in range(NT):
        sc = u_scr[i % 2]
        war = (xn2_toks[-2],) if len(xn2_toks) >= 2 else ()
        ta = p.op("vector", lambda e, i=i, sc=sc: e.tensor_scalar(
            sc[:, :], xt(i), mean[:, i:i + 1], rstd[:, i:i + 1],
            ALU.subtract, ALU.mult), (t_ln2, t_x) + war)
        tb = p.op("vector", lambda e, i=i, sc=sc: e.tensor_scalar(
            sc[:, :], sc[:, :], mean2[:, i:i + 1], rstd2[:, i:i + 1],
            ALU.subtract, ALU.mult), (ta,))
        tc = p.op("gpsimd", lambda e, sc=sc: e.tensor_mul(sc[:, :], sc[:, :], gam_sb[:, :]),
                  (tb, t_gam))
        td = p.op("vector", lambda e, i=i, sc=sc: e.tensor_add(xt(i), sc[:, :], bet_sb[:, :]),
                  (tc, t_bet))
        xn2_toks.append(td)
        mx_toks.append(p.op("vector", lambda e, i=i: e.reduce_max(
            mxc[:, i:i + 1], xt(i), axis=AX.X, apply_absolute_value=True), (td,),
            flush=True))

    tmx = p.op("vector", lambda e: e.reduce_max(red[:, :], mxc[:, :], axis=AX.X,
                                                apply_absolute_value=True), mx_toks)
    tmxf = p.fence("vector", (tmx,))
    tb1 = p.dma("gpsimd", lambda e: e.dma_start(
        out=bounce.rearrange("one p -> p one"), in_=red[:, :]), dBN, (tmxf,))
    tb2 = p.dma("gpsimd", lambda e: e.dma_start(out=redT[:, :], in_=bounce), dBN, (tb1,))
    tgm = p.op("vector", lambda e: e.reduce_max(gmx[:, :], redT[:, :], axis=AX.X), (tb2,), flush=True)
    tgf = p.fence("vector", (tgm,))
    p.dma("gpsimd", lambda e: e.dma_start(out=mx_o, in_=gmx[:, :]), dOUT, (tgf,))
    txf = p.fence("vector", tuple(xn2_toks))
    p.dma("gpsimd", lambda e: e.dma_start(out=xn2_o, in_=x_tm[:, :]), dOUT, (txf,))

    return p.build()



def build_launch2(dbg=None, stop=None):
    p = Prog()
    nc = p.nc
    x_d = nc.dram_tensor("x", [T, D], F32, kind="ExternalInput").ap()
    q_d = nc.dram_tensor("q_fm", [128, 8 * T], BF16, kind="ExternalInput").ap()
    xn2_d = nc.dram_tensor("xn2", [128, NT * 1024], F32, kind="ExternalInput").ap()
    kvs_d = nc.dram_tensor("kv_self", [128, 520], F32, kind="ExternalInput").ap()
    kvp_d = nc.dram_tensor("kv_peer", [128, 520], F32, kind="ExternalInput").ap()
    gmx_d = nc.dram_tensor("gmaxv", [1, 8], F32, kind="ExternalInput").ap()
    wo_d = nc.dram_tensor("woT_slab", [2, 1024, 512], F32, kind="ExternalInput").ap()
    bo_d = nc.dram_tensor("b_out_row", [1, 1024], F32, kind="ExternalInput").ap()
    wffn_d = nc.dram_tensor("wffnT", [1024, 1024], F32, kind="ExternalInput").ap()
    w1_d = nc.dram_tensor("w1_slab", [6, 1024, 512], F32, kind="ExternalInput").ap()
    b1_d = nc.dram_tensor("b1_fm", [128, 24], F32, kind="ExternalInput").ap()
    rw_d = nc.dram_tensor("rwT", [1024, 16], F32, kind="ExternalInput").ap()
    cb_d = nc.dram_tensor("cb_b", [128, 16], F32, kind="ExternalInput").ap()
    guw_d = nc.dram_tensor("guw_slab", [16, 2, 2, 1024, 128], F32, kind="ExternalInput").ap()
    dw_d = nc.dram_tensor("dw_slab", [16, 2, 256, 512], F32, kind="ExternalInput").ap()
    sg_d = nc.dram_tensor("sgT_slab", [16, 1024, 128], F32, kind="ExternalInput").ap()
    su_d = nc.dram_tensor("suT_slab", [16, 1024, 128], F32, kind="ExternalInput").ap()
    sd_d = nc.dram_tensor("sdT_slab", [2, 2048, 512], F32, kind="ExternalInput").ap()
    oh_d = nc.dram_tensor("onehot", [16, 2048], F32, kind="ExternalInput").ap()
    sel_d = nc.dram_tensor("selhead", [8, 16, 128], F32, kind="ExternalInput").ap()
    id_d = nc.dram_tensor("ident", [128, 128], F32, kind="ExternalInput").ap()

    xp_o = nc.dram_tensor("xp", [128, NT * 1024], F32, kind="ExternalOutput").ap()
    hs_o = nc.dram_tensor("hso", [128, 8 * T], F32, kind="ExternalOutput").ap()
    as_o = nc.dram_tensor("aso", [128, 8 * T], F32, kind="ExternalOutput").ap()
    og_o = nc.dram_tensor("ogo", [128, 8 * T], F32, kind="ExternalOutput").ap()
    hf_o = nc.dram_tensor("hfin", [128, 8], F32, kind="ExternalOutput").ap()
    dbgs_o = nc.dram_tensor("dbgs", [2, 8], F32, kind="ExternalOutput").ap()
    dbgx_o = nc.dram_tensor("dbgx", [128, NT * 1024], F32, kind="ExternalOutput").ap()
    dbgb_o = nc.dram_tensor("dbgb", [128, 8 * T], BF16, kind="ExternalOutput").ap()
    dbgz_o = nc.dram_tensor("dbgz", [16, T], F32, kind="ExternalOutput").ap()
    dbgt_o = nc.dram_tensor("dbgt", [128, 8192], BF16, kind="ExternalOutput").ap()
    dbgc_o = nc.dram_tensor("dbgc", [128, 4], F32, kind="ExternalOutput").ap()
    bounce2 = nc.dram_tensor("bounce2", [3, 128], F32).ap()

    def _fin(p_, deps, sc11_, wstat_, x_tm_, dbgs_o_, dbgx_o_, dOUT_):
        p_.dma("gpsimd", lambda e: e.dma_start(out=dbgs_o_[1:2, :], in_=sc11_[:, :]), dOUT_, deps)
        p_.dma("gpsimd", lambda e: e.dma_start(out=dbgs_o_[0:1, :], in_=wstat_[:, :]), dOUT_, deps)
        p_.dma("gpsimd", lambda e: e.dma_start(out=dbgx_o_, in_=x_tm_[:, :]), dOUT_, deps)
        return p_.build()

    x_tm = p.sb("x_tm", [128, NT * 1024], F32)
    AR1 = p.sb("AR1", [128, 12288], F32)
    AR2 = p.sb("AR2", [128, 8 * T], BF16, is_transpose_dest=True)
    AR3 = p.sb("AR3", [128, 12288], BF16)
    kvsum = AR1[:, 2048:2568]
    kv_bd = AR1[:, 2568:3080].bitcast(BF16)
    zb_sb = [AR1[:, 1024 + j * 512:1024 + (j + 1) * 512] for j in range(2)]
    sel_sb = AR1[0:16, 2048 * 0 + 4096:4096 + 1024]
    xi_scr = AR1[:, 0:1024].bitcast(I32)
    oh_r = AR1[0:16, 8192:9216].bitcast(BF16)
    us_scr = [AR1[:, 9216 + j * 256:9216 + (j + 1) * 256].bitcast(BF16) for j in range(2)]
    ig_scr = [AR3[:, j * 512:(j + 1) * 512].bitcast(F32) for j in range(2)]
    xn1_scr = None
    zseg = AR3[:, 1024:1536].bitcast(F32)
    idb_sb = AR3[:, 1536:1664]
    s1 = p.sb("s1", [128, NT], F32)
    s2 = p.sb("s2", [128, NT], F32)
    mean = p.sb("mean", [128, NT], F32)
    rstd = p.sb("rstd", [128, NT], F32)
    vscr = p.sb("vscr", [128, NT], F32)
    dump = p.sb("dump", [128, 1024], F32)
    cbw_scr = [dump[:, j * 512:(j + 1) * 512] for j in range(2)]
    bd = p.sb("bd", [128, 128], BF16)
    ones_row = p.sb("ones_row", [1, 128], F32)
    ones_rbf = p.sb("ones_rbf", [1, 128], BF16)
    bo_bf = p.sb("bo_bf", [1, 1024], BF16)
    gval = p.sb("gval", [1, 8], F32)
    sc11 = p.sb("sc11", [1, 8], F32)
    inv_b = p.sb("inv_b", [128, 1], F32)
    cw_b = p.sb("cw_b", [128, 1], F32)
    del_b = p.sb("del_b", [128, 1], F32)
    wp_scr = p.sb("wp_scr", [128, 1024], F32)
    xb_scr = [p.sb(f"xb_scr{j}", [128, 1024], BF16) for j in range(2)]
    xn1_scr = xb_scr
    sA = p.sb("sA", [128, 8], F32)
    sC = p.sb("sC", [128, 2], F32)
    red3 = p.sb("red3", [1, 128], F32)
    red3b = p.sb("red3b", [1, 128], F32)
    red3c = p.sb("red3c", [1, 128], F32)
    wstat = p.sb("wstat", [1, 8], F32)
    snap = p.sb("snap", [1, 16], F32)
    cnt_cols = p.sb("cnt_cols", [128, 16], F32)
    b1_sb = p.sb("b1_sb", [128, 24], F32)
    cb_sb = p.sb("cb_sb", [128, 16], F32)
    rw_bf = p.sb("rw_bf", [128, 128], BF16)
    scores_fm = p.sb("scores_fm", [16, T], F32)
    z_sb = scores_fm
    sc_tm = p.sb("sc_tm", [128, 256], F32)
    sbt = p.sb("sbt", [128, 256], F32)
    pairs = p.sb("pairs", [128, 384], F32)
    gsc = p.sb("gsc", [128, 64], F32)
    mins = p.sb("mins", [128, 96], F32)
    thr2 = p.sb("thr2", [128, 16], F32)
    ni_sb = p.sb("ni_sb", [128, 64], F32)
    masked = p.sb("masked", [128, 256], F32)
    selm = p.sb("selm", [128, 256], F32)
    mx8 = sbt[:, 0:128]
    wun = p.sb("wun", [128, 256], F32)
    dnm = p.sb("dnm", [128, 16], F32)
    comb = p.sb("comb", [128, 256], F32)
    C_fm = p.sb("C_fm", [16, T], BF16)
    id_sb = p.sb("id_sb", [128, 128], F32)
    carry_h = p.sb("carry_h", [128, 8], F32)
    carry_a = p.sb("carry_a", [128, 8], F32)

    ps = [p.ps(f"ps{j}") for j in range(8)]

    dIN = p.dma_sem("dIN")
    dIN2 = p.dma_sem("dIN2")
    dTR = p.dma_sem("dTR")
    dOUT = p.dma_sem("dOUT")
    dOUT2 = p.dma_sem("dOUT2")
    dBN = p.dma_sem("dBN")
    dW = [p.dma_sem(f"dW{j}") for j in range(4)]
    dXQ = p.dma_sem("dXQ")

    def xs_(i, a=0, b=1024):
        return x_tm[:, i * 1024 + a:i * 1024 + b]

    def ar2(c, a, b):
        return AR2[:, c * T + a:c * T + b]

    t_x = p.dma("sync", lambda e: e.dma_start(
        out=x_tm[:, :].rearrange("p (i f) -> p i f", i=NT),
        in_=x_d.rearrange("(i p) f -> p i f", p=128)), dIN2)
    t_q = p.dma("sync", lambda e: e.dma_start(out=AR2[:, :], in_=q_d), dIN2)
    t_kvs = p.dma("sync", lambda e: e.dma_start(out=kvsum, in_=kvs_d), dIN2)
    kvp_sb = AR1[:, 3080:3600]
    t_kvp = p.dma("sync", lambda e: e.dma_start(out=kvp_sb, in_=kvp_d), dIN2)
    t_gmx = p.dma("sync", lambda e: e.dma_start(out=gval[:, :], in_=gmx_d), dIN2)
    t_sel = p.dma("sync", lambda e: e.dma_start(
        out=sel_sb.rearrange("q (c p) -> q c p", c=8),
        in_=sel_d.transpose([1, 0, 2])), dIN2)
    t_id = p.dma("sync", lambda e: e.dma_start(out=id_sb[:, :], in_=id_d), dIN2)
    t_b1 = p.dma("sync", lambda e: e.dma_start(out=b1_sb[:, :], in_=b1_d), dIN2)
    t_cb = p.dma("sync", lambda e: e.dma_start(out=cb_sb[:, :], in_=cb_d), dIN2)
    wo_v = AR3[:, 0:8192].rearrange("p (k c) -> p k c", k=8)
    t_wo = p.dma("gpsimd", lambda e: e.dma_start(
        out=wo_v[:, :, 0:512], in_=wo_d[0].rearrange("(k p) c -> p k c", p=128)), dIN)
    t_wo2 = p.dma("gpsimd", lambda e: e.dma_start(
        out=wo_v[:, :, 512:1024], in_=wo_d[1].rearrange("(k p) c -> p k c", p=128)), dIN)
    t_bo = p.dma("gpsimd", lambda e: e.dma_start(out=bo_bf[:, :], in_=bo_d), dIN)
    t_rw = p.dma("gpsimd", lambda e: e.dma_start(
        out=rw_bf[:, :].rearrange("p (k c) -> p k c", k=8),
        in_=rw_d.rearrange("(k p) c -> p k c", p=128)), dIN)
    t_ones = p.op("vector", lambda e: e.memset(ones_row[:, :], 1.0))
    t_onesb = p.op("vector", lambda e: e.memset(ones_rbf[:, :], 1.0))

    tg1 = p.op("vector", lambda e: e.reduce_max(sc11[:, 0:1], gval[:, :], axis=AX.X), (t_gmx,))
    tg2 = p.op("vector", lambda e: e.tensor_scalar_max(sc11[:, 0:1], sc11[:, 0:1], 1e-8), (tg1,), flush=True)
    tg3 = p.op("vector", lambda e: e.tensor_scalar_mul(sc11[:, 1:2], sc11[:, 0:1], 1.0 / 127.0), (tg2,), flush=True)
    tg4 = p.op("vector", lambda e: e.reciprocal(sc11[:, 2:3], sc11[:, 1:2]), (tg3,), flush=True)
    tg5 = p.mm(ps[4][:, 0:1], lambda k: ones_row[:, :], lambda k: sc11[:, 2:3], 1,
               deps=(tg4, t_ones))
    tg6 = p.op("vector", lambda e: e.tensor_copy(inv_b[:, :], ps[4][:, 0:1]), (tg5,))
    p.evac(ps[4].name, tg6)

    tk1 = p.op("vector", lambda e: e.tensor_add(kvsum, kvsum, kvp_sb),
               (t_kvs, t_kvp))
    tk2a = p.op("vector", lambda e: e.memset(kv_bd, 0.0), (tk1,), flush=True)
    tk2_l = []
    for j in range(8):
        tk2_l.append(p.op("vector", lambda e, j=j: e.tensor_copy(
            kv_bd[0:64, 128 * j:128 * j + 64], kvsum[0:64, 64 * j:64 * (j + 1)]),
            (tk1, tk2a)))
        tk2_l.append(p.op("vector", lambda e, j=j: e.tensor_copy(
            kv_bd[64:128, 128 * j + 64:128 * (j + 1)], kvsum[64:128, 64 * j:64 * (j + 1)]),
            (tk1, tk2a)))
    tk2 = tk2_l[-1]
    all_tk2 = tuple(tk2_l)
    tbd0 = p.op("vector", lambda e: e.memset(bd[:, :], 0.0))
    tbd = [tbd0]
    for c in range(8):
        tbd.append(p.op("vector", lambda e, c=c: e.tensor_copy(
            bd[0:64, c * 16 + 2 * c:c * 16 + 2 * c + 1],
            kvsum[0:64, 512 + c:513 + c]), (tbd0, tk1)))
        tbd.append(p.op("vector", lambda e, c=c: e.tensor_copy(
            bd[64:128, c * 16 + 2 * c + 1:c * 16 + 2 * c + 2],
            kvsum[64:128, 512 + c:513 + c]), (tbd0, tk1)))

    if stop == "tg":
        return _fin(p, (tg6, tk2) + tuple(tbd), sc11, wstat, x_tm, dbgs_o, dbgx_o, dOUT)
    strips = AR3[:, 8192:12288].bitcast(F32)
    t_bf_v = AR1[:, 5120:9216].bitcast(BF16)

    p1_toks = []
    for k in range(8):
        sl = strips[:, (k % 2) * 1024:(k % 2) * 1024 + 1024]
        war = (p1_toks[-2],) if len(p1_toks) >= 2 else ()
        tl = p.dma("gpsimd", lambda e, k=k, sl=sl: e.dma_start(
            out=sl, in_=wffn_d[k * 128:(k + 1) * 128, :]), dW[0], war)
        tr = p.op("vector", lambda e, k=k, sl=sl: e.reduce_sum(
            sA[:, k:k + 1], sl, axis=AX.X, apply_absolute_value=True), (tl,))
        p1_toks.append(tr)
    tps2 = p.op("vector", lambda e: e.reduce_sum(sC[:, 0:1], sA[:, :], axis=AX.X), tuple(p1_toks), flush=True)
    tpsf = p.fence("vector", (tps2,))
    tb1 = p.dma("gpsimd", lambda e: e.dma_start(
        out=bounce2[0:1, :].rearrange("one p -> p one"), in_=sC[:, 0:1]), dBN, (tpsf,))
    tb2 = p.dma("gpsimd", lambda e: e.dma_start(out=red3[:, :], in_=bounce2[0:1, :]), dBN, (tb1,))
    tds = p.op("vector", lambda e: e.reduce_sum(wstat[:, 0:1], red3[:, :], axis=AX.X), (tb2,), flush=True)
    tdd = p.op("vector", lambda e: e.tensor_scalar_mul(
        wstat[:, 7:8], wstat[:, 0:1], 0.7 / (1024.0 * 1024.0)), (tds,), flush=True)
    tdb = p.mm(ps[4][:, 0:1], lambda k: ones_row[:, :], lambda k: wstat[:, 7:8], 1,
               deps=(tdd, t_ones))
    tdbe = p.op("vector", lambda e: e.tensor_copy(del_b[:, :], ps[4][:, 0:1]), (tdb,))
    p.evac(ps[4].name, tdbe)

    p2_last = []
    strip_war2 = []
    for k in range(8):
        sl = strips[:, (k % 2) * 1024:(k % 2) * 1024 + 1024]
        war = (strip_war2[-2],) if len(strip_war2) >= 2 else (p1_toks[-1],)
        tl = p.dma("gpsimd", lambda e, k=k, sl=sl: e.dma_start(
            out=sl, in_=wffn_d[k * 128:(k + 1) * 128, :]), dW[0], (tdbe,) + war)
        tabs = p.op("scalar", lambda e, sl=sl: e.activation(dump[:, :], sl, AF.Abs),
                    (tl,) + (((strip_war2[-2],) if len(strip_war2) >= 2 else ())))
        tmk0 = p.op("vector", lambda e, k=k: e.tensor_single_scalar(
            dump[:, :], dump[:, :], del_b[:, 0:1], ALU.is_gt), (tabs, tdbe))
        tmk = p.op("vector", lambda e, k=k: e.reduce_sum(
            cnt_cols[:, k:k + 1], dump[:, :], axis=AX.X), (tmk0,))
        tam = p.op("vector", lambda e, sl=sl: e.scalar_tensor_tensor(
            wp_scr[:, :], sl, 0.0, dump[:, :], ALU.bypass, ALU.mult), (tmk,))
        tamr = p.op("vector", lambda e, k=k: e.reduce_sum(
            cnt_cols[:, 8 + k:9 + k], wp_scr[:, :], axis=AX.X,
            apply_absolute_value=True), (tam,))
        tt = p.op("scalar", lambda e, k=k: e.sign(
            t_bf_v[:, k * 1024:(k + 1) * 1024], wp_scr[:, :]),
            (tam, tamr))
        strip_war2.append(tt)
        p2_last.append(tt)
    tc1 = p.op("vector", lambda e: e.reduce_sum(sC[:, 0:1], cnt_cols[:, 0:8], axis=AX.X), tuple(p2_last) + (tb1,))
    tc2 = p.op("vector", lambda e: e.reduce_sum(sC[:, 1:2], cnt_cols[:, 8:16], axis=AX.X), tuple(p2_last))
    tcf = p.fence("vector", (tc1, tc2))
    tb3a = p.dma("gpsimd", lambda e: e.dma_start(
        out=bounce2[1:2, :].rearrange("one p -> p one"), in_=sC[:, 0:1]), dBN, (tcf,))
    tb3 = p.dma("gpsimd", lambda e: e.dma_start(
        out=bounce2[2:3, :].rearrange("one p -> p one"), in_=sC[:, 1:2]), dBN, (tc2, tb3a))
    tb4 = p.dma("gpsimd", lambda e: e.dma_start(out=red3b[:, :], in_=bounce2[1:2, :]), dBN, (tb3,))
    tb4b = p.dma("gpsimd", lambda e: e.dma_start(out=red3c[:, :], in_=bounce2[2:3, :]), dBN, (tb4,))
    ta1 = p.op("vector", lambda e: e.reduce_sum(wstat[:, 1:2], red3b[:, :], axis=AX.X), (tb4,), flush=True)
    ta2 = p.op("vector", lambda e: e.reduce_sum(wstat[:, 2:3], red3c[:, :], axis=AX.X), (tb4b,), flush=True)
    ta3 = p.op("vector", lambda e: e.tensor_scalar_max(wstat[:, 1:2], wstat[:, 1:2], 1.0), (ta1,), flush=True)
    ta4 = p.op("vector", lambda e: e.reciprocal(wstat[:, 3:4], wstat[:, 1:2]), (ta3,), flush=True)
    ta5 = p.op("vector", lambda e: e.tensor_mul(wstat[:, 4:5], wstat[:, 2:3], wstat[:, 3:4]), (ta2, ta4), flush=True)
    ta6 = p.op("vector", lambda e: e.tensor_mul(wstat[:, 5:6], wstat[:, 4:5], sc11[:, 1:2]), (ta5, tg3), flush=True)
    ta7 = p.mm(ps[4][:, 0:1], lambda k: ones_row[:, :], lambda k: wstat[:, 5:6], 1,
               deps=(ta6, t_ones))
    ta8 = p.op("vector", lambda e: e.tensor_copy(cw_b[:, :], ps[4][:, 0:1]), (ta7,))
    p.evac(ps[4].name, ta8)

    tsnap1 = p.op("vector", lambda e: e.tensor_copy(snap[:, 0:3], sc11[:, 0:3]), (tg6,))
    tsnap2 = p.op("vector", lambda e: e.tensor_copy(snap[:, 3:4], inv_b[0:1, 0:1]), (tg6,))
    den_toks = []
    for t4 in range(4):
        tm = p.mm(ps[5][0:16, :], lambda k: bd[:, k * 16:(k + 1) * 16],
                  lambda k, t4=t4: ar2(k, t4 * 512, (t4 + 1) * 512), 8,
                  deps=tuple(tbd) + (t_q,))
        td1 = p.op("vector", lambda e, t4=t4: e.tensor_scalar_add(
            z_sb[:, t4 * 512:(t4 + 1) * 512], ps[5][0:16, :], 1e-6), (tm,))
        td2 = p.op("vector", lambda e, t4=t4: e.reciprocal(
            z_sb[:, t4 * 512:(t4 + 1) * 512], z_sb[:, t4 * 512:(t4 + 1) * 512]), (td1,))
        p.evac(ps[5].name, td2)
        den_toks.append(td2)
    all_den = tuple(den_toks)

    attn_toks = []
    zb_war = {}
    for c in range(8):
        for t4 in range(4):
            tz = p.mm(ps[4][:, :], lambda k, c=c: sel_sb[:, c * 128:(c + 1) * 128],
                      lambda k, t4=t4: z_sb[:, t4 * 512:(t4 + 1) * 512], 1,
                      deps=(den_toks[t4], t_sel))
            tze = p.op("scalar", lambda e, t4=t4: e.copy(zb_sb[t4 % 2], ps[4][:, :]),
                       (tz, zb_war.get(t4 % 2)))
            p.evac(ps[4].name, tze)

            def qkv_fn(e, c=c, t4=t4):
                return e.matmul(ps[6][:, :], kv_bd[:, c * 128:(c + 1) * 128],
                                AR2[:, c * T + t4 * 512:c * T + (t4 + 1) * 512],
                                start=True, stop=True)
            depsq = all_tk2 + (t_q,) + all_den
            if ps[6].name in p.bank_tok:
                depsq += (p.bank_tok[ps[6].name],)
            tqk = p.op("tensor", qkv_fn, depsq)
            ta = p.op("vector", lambda e, c=c, t4=t4: e.tensor_tensor(
                ar2(c, t4 * 512, (t4 + 1) * 512), ps[6][:, :], zb_sb[t4 % 2],
                ALU.mult), (tqk, tze) + all_den)
            p.evac(ps[6].name, ta)
            zb_war[t4 % 2] = ta
            attn_toks.append(ta)
    all_attn = tuple(attn_toks)

    if stop == "C":
        p.dma("gpsimd", lambda e: e.dma_start(out=dbgb_o, in_=AR2[:, :]), dOUT, all_attn)
        p.dma("gpsimd", lambda e: e.dma_start(out=dbgz_o, in_=z_sb[:, :]), dOUT, all_den)
        return _fin(p, all_attn + (tg6,), sc11, wstat, x_tm, dbgs_o, dbgx_o, dOUT)
    d_toks = []
    for ti in range(NT):
        for n2 in range(2):
            psd = ps[n2]
            tmo = p.mm(psd[:, :],
                       lambda k, ti=ti: AR2[:, k * T + ti * 128:k * T + (ti + 1) * 128],
                       lambda k, n2=n2: wo_v[:, k, n2 * 512:(n2 + 1) * 512], 8,
                       deps=all_attn + (t_wo, t_wo2, t_bo, t_onesb), stop=False,
                       extras=(lambda e, n2=n2, psd=psd: e.matmul(
                           psd[:, :], ones_rbf[:, :], bo_bf[:, n2 * 512:(n2 + 1) * 512],
                           start=False, stop=True),))
            tev = p.op("vector", lambda e, ti=ti, n2=n2, psd=psd: e.scalar_tensor_tensor(
                xs_(ti, n2 * 512, (n2 + 1) * 512), psd[:, :], 0.0,
                xs_(ti, n2 * 512, (n2 + 1) * 512), ALU.add, ALU.add), (tmo, t_x))
            p.evac(psd.name, tev)
            d_toks.append(tev)
    all_d = tuple(d_toks)

    if stop == "D":
        return _fin(p, all_d + (tg6,), sc11, wstat, x_tm, dbgs_o, dbgx_o, dOUT)

    xq_toks = []
    qt_bufs = [wp_scr, dump]
    for i in range(NT):
        sl = qt_bufs[i % 2]
        slb = xb_scr[i % 2]
        war = (xq_toks[-2],) if len(xq_toks) >= 2 else tuple(p2_last) + (tc1, tc2)
        tl = p.dma("sync", lambda e, i=i, sl=sl: e.dma_start(
            out=sl[:, :], in_=xn2_d[:, i * 1024:(i + 1) * 1024]), dXQ, war)
        tq1 = p.op("vector", lambda e, sl=sl: e.tensor_scalar(
            sl[:, :], sl[:, :], inv_b[:, 0:1], 127.0, ALU.mult, ALU.min), (tl, tg6))
        tq2 = p.op("vector", lambda e, sl=sl: e.tensor_scalar_max(sl[:, :], sl[:, :], -127.0), (tq1,))
        tq3 = p.op("vector", lambda e, sl=sl: e.tensor_copy(xi_scr, sl[:, :]), (tq2,))
        tq4 = p.op("vector", lambda e, slb=slb: e.tensor_copy(slb[:, :], xi_scr), (tq3,))
        gate = all_attn + all_d if i < 2 else ()
        tq5 = p.dma("sync", lambda e, i=i, slb=slb: e.dma_start_transpose(
            AR2[:, :].rearrange("p (c t) -> p c t", c=8)[:, :, i * 128:(i + 1) * 128],
            slb[:, :]), dTR, (tq4,) + gate)
        xq_toks.append(tq5)
    all_xq = tuple(xq_toks)

    e_toks = []
    for ti in range(NT):
        for n2 in range(2):
            psd = ps[n2]
            tmf = p.mm(psd[:, :],
                       lambda k, ti=ti: AR2[:, k * T + ti * 128:k * T + (ti + 1) * 128],
                       lambda k, n2=n2: t_bf_v[:, k * 1024 + n2 * 512:k * 1024 + (n2 + 1) * 512], 8,
                       deps=all_xq + tuple(p2_last))
            tev = p.op("vector", lambda e, ti=ti, n2=n2, psd=psd: e.scalar_tensor_tensor(
                xs_(ti, n2 * 512, (n2 + 1) * 512), psd[:, :], cw_b[:, 0:1],
                xs_(ti, n2 * 512, (n2 + 1) * 512), ALU.mult, ALU.add), (tmf, ta8))
            p.evac(psd.name, tev)
            e_toks.append(tev)
    all_e = tuple(e_toks)

    if stop == "E":
        p.dma("gpsimd", lambda e: e.dma_start(out=dbgc_o[:, 0:1], in_=cw_b[:, :], allow_slow_non_contiguous=True), dOUT, (ta8,) + all_e)
        p.dma("gpsimd", lambda e: e.dma_start(out=dbgc_o[:, 1:2], in_=inv_b[:, :], allow_slow_non_contiguous=True), dOUT, (tg6,) + all_e)
        p.dma("gpsimd", lambda e: e.dma_start(out=dbgb_o, in_=AR2[:, :]), dOUT, all_xq + all_e)
        p.dma("gpsimd", lambda e: e.dma_start(out=dbgt_o, in_=t_bf_v), dOUT, tuple(p2_last) + all_e)
        return _fin(p, all_e + (tg6, ta8), sc11, wstat, x_tm, dbgs_o, dbgx_o, dOUT)
    f_stat_toks = []
    for i in range(NT):
        dep_i = all_e + all_d
        f_stat_toks.append(p.op("vector", lambda e, i=i: e.reduce_sum(
            s1[:, i:i + 1], xs_(i), axis=AX.X), dep_i))
        f_stat_toks.append(p.op("scalar", lambda e, i=i: e.activation(
            dump[:, :], xs_(i), AF.Square, accum_out=s2[:, i:i + 1]),
            dep_i + tuple(p2_last) + (xq_toks[1], xq_toks[-1])))
    tf1 = p.op("vector", lambda e: e.tensor_scalar_mul(mean[:, :], s1[:, :], 1.0 / D), f_stat_toks)
    tf2 = p.op("vector", lambda e: e.tensor_scalar_mul(vscr[:, :], s2[:, :], 1.0 / D), f_stat_toks)
    tf3 = p.op("vector", lambda e: e.tensor_mul(s1[:, :], mean[:, :], mean[:, :]), (tf1,), flush=True)
    tf4 = p.op("vector", lambda e: e.tensor_sub(vscr[:, :], vscr[:, :], s1[:, :]), (tf2, tf3), flush=True)
    tf5 = p.op("vector", lambda e: e.tensor_scalar_add(vscr[:, :], vscr[:, :], EPS), (tf4,), flush=True)
    tf6 = p.op("scalar", lambda e: e.sqrt(s2[:, :], vscr[:, :]), (tf5,))
    tf7 = p.op("vector", lambda e: e.reciprocal(rstd[:, :], s2[:, :]), (tf6,), flush=True)

    xn1_tn = []
    xn1_toks = []
    for i in range(NT):
        slb = xn1_scr[i % 2]
        war = (xn1_toks[-2],) if len(xn1_toks) >= 2 else ()
        tn = p.op("vector", lambda e, i=i, slb=slb: e.tensor_scalar(
            slb[:, :], xs_(i), mean[:, i:i + 1], rstd[:, i:i + 1],
            ALU.subtract, ALU.mult), (tf7,) + war + ((all_xq[-2], all_xq[-1]) if i < 2 else ()))
        gate = tuple(e_toks) if i < 2 else ()
        tt = p.dma("sync", lambda e, i=i, slb=slb: e.dma_start_transpose(
            AR2[:, :].rearrange("p (c t) -> p c t", c=8)[:, :, i * 128:(i + 1) * 128],
            slb[:, :]), dTR, (tn,) + gate)
        xn1_tn.append(tn)
        xn1_toks.append(tt)
    t_xn1 = tuple(xn1_toks)
    xgate = xn1_tn[-1]

    if stop == "F":
        return _fin(p, t_xn1 + (tg6, ta8), sc11, wstat, x_tm, dbgs_o, dbgx_o, dOUT)
    sco_toks = []
    sctm_toks = []
    for ti in range(NT):
        tm = p.mm(ps[5][:, 0:16],
                  lambda k, ti=ti: AR2[:, k * T + ti * 128:k * T + (ti + 1) * 128],
                  lambda k: rw_bf[:, k * 16:(k + 1) * 16], 8,
                  deps=t_xn1 + (t_rw,))
        te = p.op("scalar", lambda e, ti=ti: e.activation(
            sc_tm[:, ti * 16:(ti + 1) * 16], ps[5][:, 0:16], AF.Sigmoid), (tm,))
        p.evac(ps[5].name, te)
        sctm_toks.append(te)
    tscf = p.fence("scalar", tuple(sctm_toks))
    tsc_early = p.dma("gpsimd", lambda e: e.dma_start(out=dbgt_o[:, 256:512], in_=sc_tm[:, :]), dBN, (tscf,))
    if stop == "H0":
        return _fin(p, tuple(sctm_toks) + (tscf,), sc11, wstat, x_tm, dbgs_o, dbgx_o, dOUT)
    sbt_toks = [p.op("vector", lambda e, ti=ti: e.tensor_add(
        sbt[:, ti * 16:(ti + 1) * 16], sc_tm[:, ti * 16:(ti + 1) * 16], cb_sb[:, :]),
        (sctm_toks[ti], t_cb), flush=(ti == 0)) for ti in range(NT)]

    def V(t):
        return t[:, :].rearrange("p (ti g i) -> p ti g i", g=4, i=4)
    PAIRS = [(0, 1), (0, 2), (0, 3), (1, 2), (1, 3), (2, 3)]
    pr_toks = [p.op("vector", lambda e, j=j, a=a, b=b: e.tensor_add(
        pairs[:, j * 64:(j + 1) * 64].rearrange("p (ti g) -> p ti g", g=4),
        V(sbt)[:, :, :, a], V(sbt)[:, :, :, b]), tuple(sbt_toks), flush=(j == 0))
        for j, (a, b) in enumerate(PAIRS)]
    tgs = p.op("vector", lambda e: e.tensor_max(gsc[:, :], pairs[:, 0:64], pairs[:, 64:128]), pr_toks[0:2], flush=True)
    for j in range(2, 6):
        tgs = p.op("vector", lambda e, j=j: e.tensor_max(
            gsc[:, :], gsc[:, :], pairs[:, j * 64:(j + 1) * 64]), (tgs, pr_toks[j]), flush=True)
    GV = gsc[:, :].rearrange("p (ti g) -> p ti g", g=4)
    mn_toks = [p.op("vector", lambda e, j=j, a=a, b=b: e.tensor_tensor(
        mins[:, j * 16:(j + 1) * 16], GV[:, :, a], GV[:, :, b], ALU.min), (tgs,), flush=(j == 0))
        for j, (a, b) in enumerate(PAIRS)]
    tt2 = p.op("vector", lambda e: e.tensor_max(thr2[:, :], mins[:, 0:16], mins[:, 16:32]), mn_toks[0:2], flush=True)
    for j in range(2, 6):
        tt2 = p.op("vector", lambda e, j=j: e.tensor_max(
            thr2[:, :], thr2[:, :], mins[:, j * 16:(j + 1) * 16]), (tt2, mn_toks[j]), flush=True)
    ni_toks = [p.op("vector", lambda e, g=g: e.tensor_tensor(
        ni_sb[:, g * 16:(g + 1) * 16], GV[:, :, g], thr2[:, :], ALU.is_lt), (tt2,), flush=(g == 0))
        for g in range(4)]
    msk_toks = []
    for g in range(4):
        for i_ in range(4):
            msk_toks.append(p.op("vector", lambda e, g=g, i_=i_: e.scalar_tensor_tensor(
                V(masked)[:, :, g, i_], ni_sb[:, g * 16:(g + 1) * 16], -1e30,
                V(sbt)[:, :, g, i_], ALU.mult, ALU.add), (ni_toks[g],), flush=(g == 0 and i_ == 0)))
    mxt_toks = [p.op("vector", lambda e, ti=ti: e.max(
        mx8[:, ti * 8:ti * 8 + 8], masked[:, ti * 16:(ti + 1) * 16]), tuple(msk_toks), flush=(ti == 0))
        for ti in range(NT)]
    sel_toks = [p.op("vector", lambda e, ti=ti: e.tensor_single_scalar(
        selm[:, ti * 16:(ti + 1) * 16], masked[:, ti * 16:(ti + 1) * 16],
        mx8[:, ti * 8 + 3:ti * 8 + 4], ALU.is_ge), (mxt_toks[ti],), flush=(ti == 0)) for ti in range(NT)]
    twu = p.op("vector", lambda e: e.tensor_mul(wun[:, :], sc_tm[:, :], selm[:, :]),
               tuple(sel_toks) + tuple(sctm_toks), flush=True)
    tdn = p.op("vector", lambda e: e.reduce_sum(
        dnm[:, :], wun[:, :].rearrange("p (ti q) -> p ti q", q=16), axis=AX.X), (twu,), flush=True)
    tdn2 = p.op("vector", lambda e: e.tensor_scalar_add(dnm[:, :], dnm[:, :], 1e-20), (tdn,), flush=True)
    tdn3 = p.op("vector", lambda e: e.reciprocal(dnm[:, :], dnm[:, :]), (tdn2,), flush=True)
    cmb_toks = [p.op("vector", lambda e, ti=ti: e.tensor_single_scalar(
        comb[:, ti * 16:(ti + 1) * 16], wun[:, ti * 16:(ti + 1) * 16],
        dnm[:, ti:ti + 1], ALU.mult), (tdn3,), flush=(ti == 0)) for ti in range(NT)]
    comb_bf = masked[:, :].bitcast(BF16)[:, 0:256]
    tcb_bf = p.op("vector", lambda e: e.tensor_copy(comb_bf, comb[:, :]),
                  tuple(cmb_toks) + tuple(sel_toks) + tuple(mxt_toks), flush=True)
    tid_bf = p.op("vector", lambda e: e.tensor_copy(idb_sb[:, :], id_sb[:, :]), (t_id,))
    cfm_toks = []
    for ti in range(NT):
        bk = (p.bank_tok.get(ps[7].name),)
        tt = p.op("tensor", lambda e, ti=ti: e.transpose(
            ps[7][:, :].bitcast(BF16)[0:16, 0:128], comb_bf[:, ti * 16:(ti + 1) * 16],
            idb_sb[:, :]), (tcb_bf, tid_bf) + bk)
        te = p.op("vector", lambda e, ti=ti: e.tensor_copy(
            C_fm[:, ti * 128:(ti + 1) * 128], ps[7][:, :].bitcast(BF16)[0:16, 0:128]), (tt,))
        p.evac(ps[7].name, te)
        cfm_toks.append(te)
    all_cfm = tuple(cfm_toks)
    tsnap3 = p.op("vector", lambda e: e.tensor_copy(snap[:, 4:7], sc11[:, 0:3]), all_cfm)

    if stop == "H":
        tfh = p.fence("vector", tuple(cmb_toks) + tuple(sctm_toks) + (twu, tdn3))
        p.dma("gpsimd", lambda e: e.dma_start(out=dbgt_o[:, 0:256], in_=comb[:, :]), dBN, (tfh,))
        p.dma("gpsimd", lambda e: e.dma_start(out=dbgt_o[:, 512:768], in_=selm[:, :]), dBN, (tfh,))
        p.dma("gpsimd", lambda e: e.dma_start(out=dbgt_o[:, 768:832], in_=gsc[:, :]), dBN, (tfh,))
        p.dma("gpsimd", lambda e: e.dma_start(out=dbgt_o[:, 832:848], in_=thr2[:, :]), dBN, (tfh,))
        p.dma("gpsimd", lambda e: e.dma_start(out=dbgt_o[:, 1024:1280], in_=sbt[:, :]), dBN, (tfh,))
        p.dma("gpsimd", lambda e: e.dma_start(out=dbgt_o[:, 1280:1536], in_=wun[:, :]), dBN, (tfh,))
        p.dma("gpsimd", lambda e: e.dma_start(out=dbgz_o[:, 0:1024], in_=C_fm[:, :].bitcast(F32)), dBN, all_cfm)
        p.dma("gpsimd", lambda e: e.dma_start(out=dbgb_o, in_=AR2[:, :]), dBN, tuple(sctm_toks) + t_xn1)
        return _fin(p, all_cfm + (tg6, ta8), sc11, wstat, x_tm, dbgs_o, dbgx_o, dOUT)
    SEG = 256
    NSEG = T // SEG
    inp_v = AR1[:, 0:2048].rearrange("p (c t) -> p c t", c=8)
    f_v = AR1[:, 2048:4096].rearrange("p (c t) -> p c t", c=8)
    og_v = [AR1[:, 4096:6144].rearrange("p (c t) -> p c t", c=8) for j in range(2)]
    h_v = [AR1[:, 6144 + j * 2048:6144 + (j + 1) * 2048].rearrange("p (c t) -> p c t", c=8)
           for j in range(2)]
    a_buf = AR1[:, 10240:12288]
    tzs = p.op("vector", lambda e: e.memset(zseg, 0.0), all_e)
    tch = p.op("vector", lambda e: e.memset(carry_h[:, :], 0.0))
    tca = p.op("vector", lambda e: e.memset(carry_a[:, :], 1.0))

    wg_bufs = [AR3[:, 1664 + j * 4096:1664 + (j + 1) * 4096] for j in range(2)]
    grp_consumer = {}
    grp_tok = {}
    prev_scan_last = None
    h_war = [None, None]
    og_war = [None, None]
    a_war = None
    ig_war = {}
    g_final = []
    for s in range(NSEG):
        m_toks = []
        scan_deps_extra = (prev_scan_last,) if prev_scan_last else ()
        def load_grp(g6, s=s):
            wbg = wg_bufs[g6 % 2]
            war_w = (grp_consumer.get(g6 % 2),)
            gate_w = (tuple(p2_last) + (xq_toks[-1],)) if (s == 0 and g6 < 2) else ()
            grp_tok[(s, g6)] = p.dma("gpsimd", lambda e, g6=g6, wbg=wbg: e.dma_start(
                out=wbg.rearrange("p (k mc) -> p k mc", k=8),
                in_=w1_d[g6].rearrange("(k p) mc -> p k mc", p=128)),
                dW[1 + (g6 % 2)], war_w + gate_w)
        load_grp(0)
        load_grp(1)
        for m in range(24):
            g6 = m // 4
            wbg = wg_bufs[g6 % 2]
            tl = grp_tok[(s, g6)]
            psd = ps[2 + (m % 2)]
            tm = p.mm(psd[:, 0:SEG],
                      lambda k, wbg=wbg, m=m: wbg.rearrange(
                          "p (k mc) -> p k mc", k=8)[:, k, (m % 4) * 128:(m % 4 + 1) * 128],
                      lambda k, s=s: ar2(k, s * SEG, (s + 1) * SEG), 8,
                      deps=(tl,) + t_xn1)
            if m % 4 == 3:
                grp_consumer[g6 % 2] = tm
                if g6 + 2 < 6:
                    load_grp(g6 + 2)
            if m < 8:
                te = p.op("scalar", lambda e, m=m, psd=psd: e.activation(
                    inp_v[:, m, :], psd[:, 0:SEG], AF.Tanh, bias=b1_sb[:, m:m + 1]),
                    (tm, t_b1) + scan_deps_extra)
                rel = te
            elif m < 16:
                c = m - 8
                ig = ig_scr[c % 2]
                te0 = p.op("scalar", lambda e, m=m, ig=ig, psd=psd: e.activation(
                    ig, psd[:, 0:SEG], AF.Sigmoid, bias=b1_sb[:, m:m + 1]),
                    (tm, t_b1, ig_war.get(c % 2)) + scan_deps_extra + (all_e if (s == 0 and m < 10) else ()))
                te1 = p.op("vector", lambda e, c=c, ig=ig: e.tensor_scalar(
                    f_v[:, c, :], ig, -1.0, 1.0, ALU.mult, ALU.add),
                    (te0,) + scan_deps_extra)
                te = p.op("vector", lambda e, c=c, ig=ig: e.tensor_mul(
                    inp_v[:, c, :], ig, inp_v[:, c, :]), (te1, m_toks[c]))
                ig_war[c % 2] = te
                rel = te0
            else:
                c = m - 16
                te = p.op("scalar", lambda e, c=c, s=s, psd=psd, m=m: e.activation(
                    og_v[s % 2][:, c, :], psd[:, 0:SEG], AF.Sigmoid, bias=b1_sb[:, m:m + 1]),
                    (tm, t_b1, og_war[s % 2]))
                rel = te
            p.evac(psd.name, rel)
            m_toks.append(te)
        scan_toks = []
        for c in range(8):
            th = p.op("vector", lambda e, c=c, s=s: e.tensor_tensor_scan(
                h_v[s % 2][:, c, :], f_v[:, c, :], inp_v[:, c, :],
                carry_h[:, c:c + 1], ALU.mult, ALU.add),
                (m_toks[8 + c], tch) + (h_war[s % 2] or ()), flush=True)
            ta_ = p.op("vector", lambda e, c=c: e.tensor_tensor_scan(
                a_buf[:, c * SEG:(c + 1) * SEG], f_v[:, c, :], zseg,
                carry_a[:, c:c + 1], ALU.mult, ALU.add),
                (m_toks[8 + c], tzs, tca) + ((a_war,) if a_war else ()), flush=True)
            scan_toks += [th, ta_]
        tcu = p.op("vector", lambda e, s=s: e.tensor_copy(
            carry_h[:, :],
            h_v[s % 2][:, :, SEG - 1:SEG].rearrange("p c one -> p (c one)")),
            tuple(scan_toks))
        tcau = p.op("vector", lambda e: e.tensor_copy(
            carry_a[:, :],
            a_buf[:, :].rearrange("p (c t) -> p c t", t=SEG)[:, :, SEG - 1:SEG]
            .rearrange("p c one -> p (c one)")), tuple(scan_toks))
        td1_ = p.dma("gpsimd", lambda e, s=s: e.dma_start(
            out=hs_o.rearrange("p (c t) -> p c t", c=8)[:, :, s * SEG:(s + 1) * SEG],
            in_=h_v[s % 2][:, :, :]), dOUT, tuple(scan_toks))
        td2_ = p.dma("gpsimd", lambda e, s=s: e.dma_start(
            out=as_o.rearrange("p (c t) -> p c t", c=8)[:, :, s * SEG:(s + 1) * SEG],
            in_=a_buf[:, :].rearrange("p (c t) -> p c t", t=SEG)), dOUT2, tuple(scan_toks))
        td3_ = p.dma("gpsimd", lambda e, s=s: e.dma_start(
            out=og_o.rearrange("p (c t) -> p c t", c=8)[:, :, s * SEG:(s + 1) * SEG],
            in_=og_v[s % 2][:, :, :]), dOUT, tuple(m_toks[16:24]))
        h_war[s % 2] = (td1_, tcu)
        og_war[s % 2] = td3_
        a_war = td2_
        prev_scan_last = scan_toks[-1]
        g_final = [td1_, td2_, td3_, tcu, tcau] + scan_toks
    t_hff = p.fence("vector", (g_final[3],))
    t_hf = p.dma("gpsimd", lambda e: e.dma_start(out=hf_o, in_=carry_h[:, :]), dOUT,
                 (t_hff,))
    g_gate = tuple(g_final[0:5]) + (t_hf,)
    tsnap4 = p.op("vector", lambda e: e.tensor_copy(snap[:, 8:11], sc11[:, 0:3]), g_gate)

    if stop == "G":
        return _fin(p, g_gate + (tg6, ta8), sc11, wstat, x_tm, dbgs_o, dbgx_o, dOUT)
    hsh_v = AR1[:, 0:8192].bitcast(BF16).rearrange("p (m t) -> p m t", m=16)
    sg_bufs = [AR3[:, j * 1024:(j + 1) * 1024] for j in range(2)]
    su_bufs = [AR3[:, 2048 + j * 1024:2048 + (j + 1) * 1024] for j in range(2)]
    sd_buf = AR3[:, 4096:12288]
    i_toks = []
    sgl_war = {}
    sd_consumer = None
    for half in range(2):
        hm_flat = []
        for m in range(16):
            sgb, sub = sg_bufs[m % 2], su_bufs[m % 2]
            gate0 = (g_gate + tuple(p2_last) + all_e + all_cfm) if (half == 0 and m < 2) else ()
            war_g = (sgl_war.get(m % 2),)
            tlg = p.dma("gpsimd", lambda e, m=m, sgb=sgb: e.dma_start(
                out=sgb.rearrange("p (k c) -> p k c", k=8),
                in_=sg_d[m].rearrange("(k p) c -> p k c", p=128)), dW[1], gate0 + war_g)
            tlu = p.dma("gpsimd", lambda e, m=m, sub=sub: e.dma_start(
                out=sub.rearrange("p (k c) -> p k c", k=8),
                in_=su_d[m].rearrange("(k p) c -> p k c", p=128)), dW[2], gate0 + war_g)
            mm_pair = []
            for tt_ in range(2):
                t4 = half * 2 + tt_
                tmg = p.mm(ps[0][:, :],
                           lambda k, sgb=sgb: sgb.rearrange("p (k c) -> p k c", k=8)[:, k, :],
                           lambda k, t4=t4: ar2(k, t4 * 512, (t4 + 1) * 512), 8,
                           deps=(tlg,) + t_xn1)
                tmu = p.mm(ps[1][:, :],
                           lambda k, sub=sub: sub.rearrange("p (k c) -> p k c", k=8)[:, k, :],
                           lambda k, t4=t4: ar2(k, t4 * 512, (t4 + 1) * 512), 8,
                           deps=(tlu,) + t_xn1)
                tsl = p.op("scalar", lambda e, m=m, tt_=tt_: e.activation(
                    hsh_v[:, m, tt_ * 512:(tt_ + 1) * 512], ps[0][:, :], AF.Silu),
                    (tmg,) + (gate0 if tt_ == 0 else ()))
                p.evac(ps[0].name, tsl)
                tue = p.op("vector", lambda e, tt_=tt_: e.tensor_copy(
                    us_scr[tt_], ps[1][:, :]), (tmu,))
                p.evac(ps[1].name, tue)
                tmul = p.op("vector", lambda e, m=m, tt_=tt_: e.tensor_mul(
                    hsh_v[:, m, tt_ * 512:(tt_ + 1) * 512],
                    hsh_v[:, m, tt_ * 512:(tt_ + 1) * 512], us_scr[tt_]),
                    (tsl, tue))
                mm_pair.append(tmul)
            sgl_war[m % 2] = mm_pair[-1]
            hm_flat += mm_pair
        for n in range(2):
            tld = p.dma("gpsimd", lambda e, n=n: e.dma_start(
                out=sd_buf.rearrange("p (k c) -> p k c", k=16),
                in_=sd_d[n].rearrange("(k p) c -> p k c", p=128)), dW[3],
                ((sd_consumer,) if sd_consumer else (g_gate if half == 0 else ())))
            for mt in range(8):
                ti = half * 8 + mt
                psd = ps[2 + (mt % 2)]
                tmd = p.mm(psd[:, :],
                           lambda k, mt=mt: hsh_v[:, k, mt * 128:(mt + 1) * 128],
                           lambda k: sd_buf.rearrange("p (k c) -> p k c", k=16)[:, k, :],
                           16, deps=(tld,) + tuple(hm_flat))
                tev = p.op("vector", lambda e, ti=ti, n=n, psd=psd: e.scalar_tensor_tensor(
                    xs_(ti, n * 512, (n + 1) * 512), psd[:, :], 0.0,
                    xs_(ti, n * 512, (n + 1) * 512), ALU.add, ALU.add),
                    (tmd, xgate))
                p.evac(psd.name, tev)
                i_toks.append(tev)
                sd_consumer = tmd
    all_i = tuple(i_toks)

    if stop == "I":
        return _fin(p, all_i + (tg6, ta8), sc11, wstat, x_tm, dbgs_o, dbgx_o, dOUT)
    t_ohr = p.dma("gpsimd", lambda e: e.dma_start(out=oh_r, in_=oh_d), dIN, g_gate)
    he_v = AR1[:, 0:8192].bitcast(BF16).rearrange("p (q t) -> p q t", q=32)
    guw_bufs = [AR3[:, j * 4096:(j + 1) * 4096] for j in range(2)]
    dw_bufs = [AR3[:, 8192 + j * 2048:8192 + (j + 1) * 2048] for j in range(2)]
    j_evac_toks = []
    gw_war = {}
    uw_war = {}
    dw_war = {}
    cb_war = {}
    he_war = None
    for t4 in range(4):
        p1_he = {}
        for ex in range(16):
            gwb = guw_bufs[ex % 2]
            gate0 = (all_i + (sd_consumer,)) if (t4 == 0 and ex < 2) else ()
            tlg = p.dma("gpsimd", lambda e, ex=ex, gwb=gwb: e.dma_start(
                out=gwb.rearrange("p (gu hc k c) -> p gu hc k c", gu=2, hc=2, k=8),
                in_=guw_d[ex].rearrange("gu hc (k p) c -> p gu hc k c", p=128)),
                dW[1 + (ex % 2)], gate0 + (gw_war.get(ex % 2),))
            tlu = tlg
            tcbm = p.mm(ps[4][:, :], lambda k, ex=ex: oh_r[:, ex * 128:(ex + 1) * 128],
                        lambda k, t4=t4: C_fm[:, t4 * 512:(t4 + 1) * 512], 1,
                        deps=all_cfm + (t_ohr,))
            tcbe = p.op("scalar", lambda e, ex=ex: e.copy(cbw_scr[ex % 2], ps[4][:, :]),
                        (tcbm, cb_war.get(ex % 2), f_stat_toks[-1]))
            p.evac(ps[4].name, tcbe)
            glast = None
            for hc in range(2):
                q_ = ex * 2 + hc
                tmg = p.mm(ps[0][:, :],
                           lambda k, gwb=gwb, hc=hc: gwb.rearrange(
                               "p (gu hc k c) -> p gu hc k c", gu=2, hc=2, k=8)[:, 0, hc, k, :],
                           lambda k, t4=t4: ar2(k, t4 * 512, (t4 + 1) * 512), 8,
                           deps=(tlg,) + t_xn1)
                tmu = p.mm(ps[1][:, :],
                           lambda k, gwb=gwb, hc=hc: gwb.rearrange(
                               "p (gu hc k c) -> p gu hc k c", gu=2, hc=2, k=8)[:, 1, hc, k, :],
                           lambda k, t4=t4: ar2(k, t4 * 512, (t4 + 1) * 512), 8,
                           deps=(tlu,) + t_xn1)
                tsl = p.op("scalar", lambda e, q_=q_: e.activation(
                    he_v[:, q_, :], ps[0][:, :], AF.Silu),
                    (tmg,) + ((he_war,) if he_war and t4 > 0 else ())
                    + (gate0 if (hc == 0 and ex < 2) else ()))
                p.evac(ps[0].name, tsl)
                tue = p.op("vector", lambda e, hc=hc: e.tensor_copy(
                    us_scr[hc], ps[1][:, :]), (tmu,))
                p.evac(ps[1].name, tue)
                tm1 = p.op("vector", lambda e, q_=q_, hc=hc: e.tensor_mul(
                    he_v[:, q_, :], he_v[:, q_, :], us_scr[hc]), (tsl, tue))
                tm2 = p.op("vector", lambda e, q_=q_, ex=ex: e.tensor_mul(
                    he_v[:, q_, :], he_v[:, q_, :], cbw_scr[ex % 2]), (tm1, tcbe), flush=True)
                p1_he[q_] = tm2
                glast = tm2
            gw_war[ex % 2] = glast
            uw_war[ex % 2] = glast
            cb_war[ex % 2] = glast
        grp_toks = {}
        for ex in range(16):
            dwb = dw_bufs[ex % 2]
            tld = p.dma("gpsimd", lambda e, ex=ex, dwb=dwb: e.dma_start(
                out=dwb.rearrange("p (n k c) -> p n k c", n=2, k=2),
                in_=dw_d[ex].rearrange("n (k p) c -> p n k c", p=128)), dW[3],
                (dw_war.get(ex % 2),) + ((all_i[-1],) if t4 == 0 and ex < 2 else ()))
            glast = None
            for mt in range(4):
                for n2 in range(2):
                    psj = ps[mt * 2 + n2]

                    def dfn(e, ex=ex, mt=mt, n2=n2, dwb=dwb, t4=t4):
                        last = None
                        for hc in range(2):
                            last = e.matmul(
                                ps[mt * 2 + n2][:, :],
                                he_v[:, ex * 2 + hc, mt * 128:(mt + 1) * 128],
                                dwb.rearrange("p (n k c) -> p n k c", n=2, k=2)[:, n2, hc, :],
                                start=(ex == 0 and hc == 0), stop=(ex == 15 and hc == 1))
                        return last
                    deps = (tld, p1_he[ex * 2], p1_he[ex * 2 + 1])
                    if ex == 0 and psj.name in p.bank_tok:
                        deps += (p.bank_tok[psj.name],)
                    tokd = p.op("tensor", dfn, deps)
                    grp_toks[(mt, n2)] = tokd
                    glast = tokd
            dw_war[ex % 2] = glast
        he_war = None
        ev_this = []
        for mt in range(4):
            for n2 in range(2):
                ti = t4 * 4 + mt
                psj = ps[mt * 2 + n2]
                tev = p.op("vector", lambda e, ti=ti, n2=n2, psj=psj: e.scalar_tensor_tensor(
                    xs_(ti, n2 * 512, (n2 + 1) * 512), psj[:, :], 0.0,
                    xs_(ti, n2 * 512, (n2 + 1) * 512), ALU.add, ALU.add),
                    (grp_toks[(mt, n2)], xgate))
                p.evac(psj.name, tev)
                j_evac_toks.append(tev)
                ev_this.append(tev)
        he_war = ev_this[-1]

    tsnap5 = p.op("vector", lambda e: e.tensor_copy(snap[:, 11:14], sc11[:, 0:3]), tuple(j_evac_toks))
    p.dma("gpsimd", lambda e: e.dma_start(out=dbgs_o[0:1, :], in_=wstat[:, :]), dBN, (ta8,))
    p.dma("gpsimd", lambda e: e.dma_start(out=dbgs_o[1:2, :], in_=snap[:, 0:8]), dBN, (tsnap1, tsnap2, tsnap3, tsnap4, tsnap5))
    p.dma("gpsimd", lambda e: e.dma_start(out=dbgx_o[0:1, 16:32], in_=snap[:, :]), dBN, (tsnap1, tsnap2, tsnap3, tsnap4, tsnap5))
    if dbg == "postD":
        p.dma("gpsimd", lambda e: e.dma_start(out=dbgx_o, in_=x_tm[:, :]), dBN, all_d)
    elif dbg == "postE":
        p.dma("gpsimd", lambda e: e.dma_start(out=dbgx_o, in_=x_tm[:, :]), dBN, all_e)
    elif dbg == "postI":
        p.dma("gpsimd", lambda e: e.dma_start(out=dbgx_o, in_=x_tm[:, :]), dBN, all_i)
    else:
        p.dma("gpsimd", lambda e: e.dma_start(out=dbgx_o[:, 0:16], in_=x_tm[:, 0:16]), dBN, all_d)
    p.dma("gpsimd", lambda e: e.dma_start(out=xp_o, in_=x_tm[:, :]), dOUT,
          tuple(j_evac_toks) + all_i + all_d + all_e)
    return p.build()


def run_launch2(ws, xs, l1res, kv_pairs, gmaxv, cores=None, dbg=None, stop=None):
    nc = build_launch2(dbg, stop)
    shared = dict(woT_slab=ws["woT_slab"], b_out_row=ws["b_out_row"],
                  wffnT=ws["wffnT"], w1_slab=ws["w1_slab"], b1_fm=ws["b1_fm"],
                  rwT=ws["rwT"], cb_b=ws["cb_b"], guw_slab=ws["guw_slab"],
                  dw_slab=ws["dw_slab"],
                  sgT_slab=ws["sgT_slab"], suT_slab=ws["suT_slab"],
                  sdT_slab=ws["sdT_slab"], onehot=ws["onehot"],
                  selhead=ws["selhead"], ident=ws["ident"])
    in_maps = []
    for c in range(NCORE):
        in_maps.append(dict(
            shared, x=xs[c], q_fm=np.asarray(l1res[c]["q_fm"]),
            xn2=np.asarray(l1res[c]["xn2"]),
            kv_self=np.asarray(l1res[c]["kvp"]),
            kv_peer=np.asarray(l1res[c ^ 1]["kvp"]),
            gmaxv=gmaxv))
    return _run(nc, in_maps)



def build_launch3():
    p = Prog()
    nc = p.nc
    xp_d = nc.dram_tensor("xpf", [128, 8 * T], F32, kind="ExternalInput").ap()
    h_d = nc.dram_tensor("hso", [128, 8 * T], F32, kind="ExternalInput").ap()
    a_d = nc.dram_tensor("aso", [128, 8 * T], F32, kind="ExternalInput").ap()
    og_d = nc.dram_tensor("ogo", [128, 8 * T], F32, kind="ExternalInput").ap()
    cr_d = nc.dram_tensor("carry", [128, 8], F32, kind="ExternalInput").ap()
    x2_o = nc.dram_tensor("x2f", [128, 8 * T], F32, kind="ExternalOutput").ap()

    xp_c = [p.sb(f"xp{j}", [128, T], F32) for j in range(2)]
    h_c = [p.sb(f"h{j}", [128, T], F32) for j in range(2)]
    a_c = [p.sb(f"a{j}", [128, T], F32) for j in range(2)]
    og_c = [p.sb(f"og{j}", [128, T], F32) for j in range(2)]
    t_c = [p.sb(f"t{j}", [128, T], F32) for j in range(2)]
    cr = p.sb("cr", [128, 8], F32)
    dI = [p.dma_sem(f"dI{j}") for j in range(4)]
    dO = p.dma_sem("dO")

    t_cr = p.dma("sync", lambda e: e.dma_start(out=cr[:, :], in_=cr_d), dI[0])
    prev = {}
    for c in range(8):
        j = c % 2
        war = (prev.get(j),)
        tl1 = p.dma("sync", lambda e, c=c, j=j: e.dma_start(
            out=xp_c[j][:, :], in_=xp_d[:, c * T:(c + 1) * T]), dI[0], war)
        tl2 = p.dma("sync", lambda e, c=c, j=j: e.dma_start(
            out=h_c[j][:, :], in_=h_d[:, c * T:(c + 1) * T]), dI[1], war)
        tl3 = p.dma("gpsimd", lambda e, c=c, j=j: e.dma_start(
            out=a_c[j][:, :], in_=a_d[:, c * T:(c + 1) * T]), dI[2], war)
        tl4 = p.dma("gpsimd", lambda e, c=c, j=j: e.dma_start(
            out=og_c[j][:, :], in_=og_d[:, c * T:(c + 1) * T]), dI[3], war)
        t1 = p.op("vector", lambda e, c=c, j=j: e.scalar_tensor_tensor(
            h_c[j][:, :], a_c[j][:, :], cr[:, c:c + 1], h_c[j][:, :],
            ALU.mult, ALU.add), (tl2, tl3, t_cr))
        t2 = p.op("scalar", lambda e, j=j: e.activation(
            t_c[j][:, :], h_c[j][:, :], AF.Tanh), (t1, prev.get(j)))
        t3 = p.op("vector", lambda e, j=j: e.tensor_mul(
            t_c[j][:, :], t_c[j][:, :], og_c[j][:, :]), (t2, tl4))
        t4 = p.op("vector", lambda e, j=j: e.tensor_add(
            xp_c[j][:, :], xp_c[j][:, :], t_c[j][:, :]), (t3, tl1))
        to = p.dma("gpsimd", lambda e, c=c, j=j: e.dma_start(
            out=x2_o[:, c * T:(c + 1) * T], in_=xp_c[j][:, :]), dO, (t4,))
        prev[j] = to
    return p.build()


_NC_CACHE = {}


def _get_nc(name, builder):
    if name not in _NC_CACHE:
        _NC_CACHE[name] = builder()
    return _NC_CACHE[name]


def _run(nc, in_maps):
    res = run_bass_kernel_spmd(nc, in_maps, core_ids=list(range(NCORE)))
    return res.results


def kernel(**inputs):
    ws, xs = prep_host(inputs)

    nc1 = _get_nc("l1", build_launch1)
    sh1 = dict(wq_slab=ws["wq_slab"], wkv_slab=ws["wkv_slab"],
               gamma_b=ws["gamma_b"], beta_b=ws["beta_b"])
    r1 = _run(nc1, [dict(sh1, x=xs[c]) for c in range(NCORE)])

    gmaxv = np.ascontiguousarray(
        np.array([[np.asarray(r1[c]["maxloc"])[0, 0] for c in range(NCORE)]],
                 np.float32))

    nc2 = _get_nc("l2", build_launch2)
    sh2 = dict(woT_slab=ws["woT_slab"], b_out_row=ws["b_out_row"],
               wffnT=ws["wffnT"], w1_slab=ws["w1_slab"], b1_fm=ws["b1_fm"],
               rwT=ws["rwT"], cb_b=ws["cb_b"], guw_slab=ws["guw_slab"],
               dw_slab=ws["dw_slab"],
               sgT_slab=ws["sgT_slab"], suT_slab=ws["suT_slab"],
               sdT_slab=ws["sdT_slab"], onehot=ws["onehot"],
               selhead=ws["selhead"], ident=ws["ident"])
    in2 = [dict(sh2, x=xs[c], q_fm=np.asarray(r1[c]["q_fm"]),
                xn2=np.asarray(r1[c]["xn2"]),
                kv_self=np.asarray(r1[c]["kvp"]),
                kv_peer=np.asarray(r1[c ^ 1]["kvp"]),
                gmaxv=gmaxv) for c in range(NCORE)]
    r2 = _run(nc2, in2)

    in3 = []
    for c in range(NCORE):
        xp = np.asarray(r2[c]["xp"])
        xpt = np.ascontiguousarray(
            xp.reshape(128, NT, 1024).transpose(2, 0, 1))
        xp_fm = np.ascontiguousarray(
            xpt.reshape(8, 128, 128, NT).transpose(1, 0, 3, 2).reshape(128, 8 * T))
        if c % 2 == 1:
            carry = np.ascontiguousarray(np.asarray(r2[c - 1]["hfin"], np.float32))
        else:
            carry = np.zeros((128, 8), np.float32)
        in3.append(dict(xpf=xp_fm, hso=np.asarray(r2[c]["hso"]),
                        aso=np.asarray(r2[c]["aso"]), ogo=np.asarray(r2[c]["ogo"]),
                        carry=carry))

    nc3 = _get_nc("l3", build_launch3)
    r3 = _run(nc3, in3)

    out = np.zeros((NCORE * T, D), np.float32)
    for c in range(NCORE):
        x2 = np.asarray(r3[c]["x2f"])
        blk = x2.reshape(128, 8, T).transpose(2, 1, 0).reshape(T, D)
        out[c * T:(c + 1) * T] = blk
    return out.reshape(4, 4096, 1024)


def run_launch1(ws, xs):
    nc = build_launch1()
    shared = dict(wq_slab=ws["wq_slab"], wkv_slab=ws["wkv_slab"],
                  gamma_b=ws["gamma_b"], beta_b=ws["beta_b"])
    in_maps = [dict(shared, x=xs[c]) for c in range(NCORE)]
    return _run(nc, in_maps)



